# version 17
# baseline (speedup 1.0000x reference)
import contextlib
import math
import numpy as np
import ml_dtypes
import concourse.bass as bass
import concourse.mybir as mybir
import bass_rust
from concourse.bass_utils import run_bass_kernel_spmd

F32 = mybir.dt.float32
BF16 = mybir.dt.bfloat16
AF = mybir.ActivationFunctionType
ALU = mybir.AluOpType
AX = mybir.AxisListType
BFD = ml_dtypes.bfloat16

NCORES = 8
L = 4096
D = 1024
NST = 32
CC = 64
NCH = L // CC
HEADS = 16
DH = 64
WIN = 256
DI = 2730
DQKV = 3072
D1 = 5460
TW = 512
NTILE = L // TW
GW = 257 + L + 27
DEPTH = 6
NEGBIG = -1.0e38
LN8 = math.log(8.0)

ENGS = ("sync", "tensor", "vector", "scalar", "gpsimd")
CONV_GROUPS = 4
CONV_PH = 15


def vp(*dims):
    return bass_rust.VecI64Pair([list(d) for d in dims])


def apc(base_ap, offset_elems, dims):
    a = base_ap.copy()
    a.offset = offset_elems
    a.ap = vp(*dims)
    return a


class Prog:
    def __init__(self, nc, stack):
        self.nc = nc
        self.stream = {e: [] for e in ENGS}
        self.sem = {e: stack.enter_context(nc.semaphore(f"s_{e}")) for e in ENGS}
        self.cnt = {e: 0 for e in ENGS}
        self.last = {}
        self.water = {e: {o: 0 for o in ENGS} for e in ENGS}

    def op(self, eng, fn, inc=0, waits=(), dma=False):
        wl = []
        for w in waits:
            if w is None:
                continue
            we, wv = w
            if wv > self.water[eng][we]:
                self.water[eng][we] = wv
                wl.append((we, wv))
        rec = {"fn": fn, "wl": tuple(wl), "inc": inc, "dma": dma}
        self.stream[eng].append(rec)
        self.last[eng] = rec
        if inc:
            self.cnt[eng] += inc
        return (eng, self.cnt[eng])

    def dma(self, out, in_, waits=(), eng="sync", **kw):
        return self.op(eng, lambda h: h.dma_start(out=out, in_=in_, **kw), inc=16,
                       waits=waits, dma=True)

    def mark(self, eng):
        rec = self.last.get(eng)
        if rec is not None and not rec["inc"]:
            rec["inc"] = 16 if rec["dma"] else 1
            self.cnt[eng] += rec["inc"]
        return (eng, self.cnt[eng])

    def barrier(self, excl=("gpsimd",)):
        marks = [self.mark(e) for e in ENGS if e not in excl]
        out = []
        for e in ENGS:
            if e in excl:
                continue
            out.append(self.op(e, lambda h: h.nop(), inc=1, waits=marks))
        return out

    def tt(self, out, a, b, op, waits=(), inc=0, eng="vector"):
        return self.op(eng, lambda v: v.tensor_tensor(out=out, in0=a, in1=b, op=op),
                       inc=inc, waits=waits)

    def ts(self, out, a, s1, op0, s2=None, op1=None, waits=(), inc=0, eng="vector",
           accum_out=None):
        kw = {}
        if op1 is not None:
            kw["op1"] = op1
        if accum_out is not None:
            kw["accum_out"] = accum_out
        return self.op(eng, lambda v: v.tensor_scalar(out=out, in0=a, scalar1=s1,
                       scalar2=s2, op0=op0, **kw), inc=inc, waits=waits)

    def stt(self, out, a, s, b, op0, op1, waits=(), inc=0):
        return self.op("vector", lambda v: v.scalar_tensor_tensor(out=out, in0=a,
                       scalar=s, in1=b, op0=op0, op1=op1), inc=inc, waits=waits)

    def cp(self, out, a, waits=(), inc=0, eng="vector"):
        if eng == "scalar":
            return self.op(eng, lambda h: h.copy(out=out, in_=a), inc=inc,
                           waits=waits)
        return self.op(eng, lambda v: v.tensor_copy(out=out, in_=a), inc=inc,
                       waits=waits)

    def mset(self, ap, val, waits=(), inc=0, eng="vector"):
        return self.op(eng, lambda v: v.memset(ap, val), inc=inc, waits=waits)

    def red(self, out, a, op=ALU.max, negate=False, waits=(), inc=1):
        def fn(v):
            return v.tensor_reduce(out=out, in_=a, axis=AX.X, op=op, negate=negate)
        return self.op("vector", fn, inc=inc, waits=waits)

    def act(self, out, a, func, bias=0.0, scale=1.0, accum_out=None, waits=(), inc=0):
        kw = {}
        if accum_out is not None:
            kw["accum_out"] = accum_out
        return self.op("scalar", lambda s: s.activation(out=out, in_=a, func=func,
                       bias=bias, scale=scale, **kw), inc=inc, waits=waits)

    def mm(self, out, lhsT, rhs, start, stop, waits=(), inc=0, tile_position=None):
        kw = {}
        if tile_position is not None:
            kw["tile_position"] = tile_position
        return self.op("tensor", lambda t: t.matmul(out=out, lhsT=lhsT, rhs=rhs,
                       start=start, stop=stop, **kw), inc=inc, waits=waits)

    def tr(self, out, in_, ident, waits=(), inc=0):
        return self.op("tensor", lambda t: t.transpose(out=out, in_=in_,
                       identity=ident), inc=inc, waits=waits)



def host_constants():
    c = {}
    c["identb"] = np.eye(128, dtype=BFD)
    c["identf"] = np.eye(128, dtype=np.float32)
    perm = np.zeros((128, 128), dtype=np.float32)
    for b in (0, 64):
        for m in range(32):
            perm[b + m + 32, b + m] = -1.0
            perm[b + m, b + m + 32] = 1.0
    c["permq"] = perm.astype(BFD)
    i = np.arange(128)[:, None]
    j = np.arange(128)[None, :]
    mask = np.zeros((128, 256), dtype=np.float32)
    mask[:, 0:128] = np.where(j < i, NEGBIG, 0.0)
    mask[:, 128:256] = np.where(j > i, NEGBIG, 0.0)
    c["maskt"] = mask.astype(BFD)
    half = np.arange(0, DH, 2, dtype=np.float32)
    inv_freq = 1.0 / 10000.0 ** (half / DH)
    t = np.arange(L, dtype=np.float32)
    freqs = t[:, None] * inv_freq[None, :]
    freqs = np.concatenate([freqs, freqs], -1)
    base_scale = (half + 0.4 * DH) / (1.4 * DH)
    power = (t - L // 2) / (WIN // 2)
    scale = base_scale[None, :] ** power[:, None]
    scale = np.concatenate([scale, scale], -1)
    cos, sin = np.cos(freqs), np.sin(freqs)
    rep = lambda a: np.concatenate([a.T, a.T], axis=0).astype(BFD)
    c["csq"] = rep(cos * scale)
    c["snq"] = rep(sin * scale)
    c["csk"] = rep(cos / scale)
    c["snk"] = rep(sin / scale)
    c["ones128"] = np.ones((128, 128), dtype=BFD)
    for ch in range(8):
        m = np.zeros((128, 16), dtype=np.float32)
        m[0:64, 2 * ch] = 1.0
        m[64:128, 2 * ch + 1] = 1.0
        c[f"bd16_{ch}"] = m.astype(BFD)
        q = np.zeros((16, 128), dtype=np.float32)
        q[2 * ch, 0:64] = 1.0
        q[2 * ch + 1, 64:128] = 1.0
        c[f"bcp_{ch}"] = q.astype(BFD)
    return c



def build_tables(nc, p, stack, dr, gbufh, vbufh, ebufh, Lr_t, Li_t, cst):
    es = contextlib.ExitStack()
    sbt = lambda n, s, d: es.enter_context(nc.sbuf_tensor(n, list(s), d))
    pst = lambda n, s, d: es.enter_context(nc.psum_tensor(n, list(s), d))

    prm = sbt("tb_prm", (128, 4360), F32)
    ldt_v = prm[:, 0:8]
    dtrep = prm[:, 8:264].rearrange("p (c n) -> p c n", n=NST)
    arr = prm[:, 264:520].rearrange("p (c n) -> p c n", n=NST)
    aii = prm[:, 520:776].rearrange("p (c n) -> p c n", n=NST)
    dtar = prm[:, 776:1032].rearrange("p (c n) -> p c n", n=NST)
    dtai = prm[:, 1032:1288].rearrange("p (c n) -> p c n", n=NST)
    lamr = prm[:, 1288:1544].rearrange("p (c n) -> p c n", n=NST)
    lami = prm[:, 1544:1800].rearrange("p (c n) -> p c n", n=NST)
    l64r = prm[:, 1800:2056].rearrange("p (c n) -> p c n", n=NST)
    l64i = prm[:, 2056:2312].rearrange("p (c n) -> p c n", n=NST)
    crp = prm[:, 2312:2568].rearrange("p (c n) -> p c n", n=NST)
    cip = prm[:, 2568:2568 + 256].rearrange("p (c n) -> p c n", n=NST)
    t1 = prm[:, 2824:2824 + 256].rearrange("p (c n) -> p c n", n=NST)
    t2 = prm[:, 3080:3080 + 256].rearrange("p (c n) -> p c n", n=NST)
    t3 = prm[:, 3336:3336 + 256].rearrange("p (c n) -> p c n", n=NST)
    halfpi = sbt("tb_hp", (128, 1), F32)
    hp_m = p.mset(halfpi[:], math.pi / 2, inc=1)

    ld = [p.dma(ldt_v, apc(dr["log_dt"], 0, [(1, 128), (128, 8)]),
                allow_slow_non_contiguous=True)]
    for name, dst in (("log_A_real", arr), ("A_imag", aii),
                      ("C_real", crp), ("C_imag", cip)):
        ld.append(p.dma(dst, apc(dr[name], 0, [(NST, 128), (128 * NST, 8), (1, NST)])))
    lds = p.mark("sync")

    p.act(ldt_v, ldt_v, AF.Exp, waits=[lds])
    sc0 = p.mark("scalar")
    for j in range(NST):
        p.cp(dtrep[:, :, j], ldt_v, waits=[sc0])
    p.act(prm[:, 264:520], prm[:, 264:520], AF.Exp, waits=[lds])
    sc1 = p.mark("scalar")
    p.ts(prm[:, 264:520], prm[:, 264:520], -1.0, ALU.mult, waits=[sc1])
    p.tt(dtar, arr, dtrep, ALU.mult)
    p.tt(dtai, aii, dtrep, ALU.mult)
    src = dtai
    for _ in range(2):
        p.ts(t2, src, math.pi, ALU.is_gt)
        p.stt(t1, t2, -2 * math.pi, src, ALU.mult, ALU.add)
        src = t1
    vm = p.mark("vector")
    ab_m = p.act(prm[:, 4104:4360], prm[:, 2824:3080], AF.Abs, waits=[vm, hp_m],
                 inc=1)
    p.ts(prm[:, 3592:3848], prm[:, 4104:4360], -1.0, ALU.mult,
         s2=math.pi / 2, op1=ALU.add, waits=[ab_m])
    cv_m = p.mark("vector")
    p.act(prm[:, 3592:3848], prm[:, 3592:3848], AF.Sin, waits=[cv_m])
    p.act(prm[:, 3848:4104], prm[:, 2824:3080], AF.Sin, waits=[vm])
    p.act(prm[:, 1288:1544], prm[:, 776:1032], AF.Exp, waits=[vm])
    sc2 = p.mark("scalar")
    p.tt(lami, lamr, prm[:, 3848:4104].rearrange("p (c n) -> p c n", n=NST),
         ALU.mult, waits=[sc2])
    p.tt(lamr, lamr, prm[:, 3592:3848].rearrange("p (c n) -> p c n", n=NST),
         ALU.mult)
    p.tt(t1, arr, arr, ALU.mult)
    p.stt(t2, aii, 1.0, aii, ALU.mult, ALU.mult)
    p.tt(t1, t1, t2, ALU.add)
    p.op("vector", lambda v: v.reciprocal(out=qtmp[:, 512:768], in_=prm[:, 2824:3080]))
    p.ts(t2, lamr, -1.0, ALU.add)
    p.tt(t3, t2, arr, ALU.mult)
    qtmp = sbt("tb_q", (128, 3 * 256), F32)
    qr = qtmp[:, 0:256].rearrange("p (c n) -> p c n", n=NST)
    qi = qtmp[:, 256:512].rearrange("p (c n) -> p c n", n=NST)
    vinv = qtmp[:, 512:768].rearrange("p (c n) -> p c n", n=NST)
    p.tt(qr, lami, aii, ALU.mult)
    p.tt(qr, qr, t3, ALU.add)
    p.tt(qr, qr, vinv, ALU.mult)
    p.tt(t3, lami, arr, ALU.mult)
    p.tt(qi, t2, aii, ALU.mult)
    p.tt(qi, t3, qi, ALU.subtract)
    p.tt(qi, qi, vinv, ALU.mult)
    p.tt(t1, crp, qr, ALU.mult)
    p.tt(t2, cip, qi, ALU.mult)
    p.tt(t3, crp, qi, ALU.mult)
    p.tt(qi, cip, qr, ALU.mult)
    p.tt(crp, t1, t2, ALU.subtract)
    p.tt(cip, t3, qi, ALU.add)
    p.ts(prm[:, 2312:2568], prm[:, 2312:2568], 2.0, ALU.mult)
    p.ts(prm[:, 2568:2824], prm[:, 2568:2824], 2.0, ALU.mult)
    p.cp(prm[:, 3080:3336], qtmp[:, 0:256])
    p.cp(prm[:, 3336:3592], qtmp[:, 512:768])
    cdone = p.mark("vector")

    POW = sbt("tb_POW", (128, 2 * 2 * 65 * NST), F32)
    pr3 = POW[:, 0:4160].rearrange("p (c u n) -> p c u n", c=2, n=NST)
    pi3 = POW[:, 4160:8320].rearrange("p (c u n) -> p c u n", c=2, n=NST)
    REP = sbt("tb_REP", (128, 2 * 2 * 64 * NST), F32)
    rr3 = REP[:, 0:4096].rearrange("p (c u n) -> p c u n", c=2, n=NST)
    ri3 = REP[:, 4096:8192].rearrange("p (c u n) -> p c u n", c=2, n=NST)
    SCR = sbt("tb_SCR", (128, 2 * 2 * 64 * NST), F32)
    sa3 = SCR[:, 0:4096].rearrange("p (c u n) -> p c u n", c=2, n=NST)
    sb3 = SCR[:, 4096:8192].rearrange("p (c u n) -> p c u n", c=2, n=NST)
    CRE = sbt("tb_CRE", (128, 2 * 2 * 64 * NST), F32)
    cr3 = CRE[:, 0:4096].rearrange("p (c u n) -> p c u n", c=2, n=NST)
    ci3 = CRE[:, 4096:8192].rearrange("p (c u n) -> p c u n", c=2, n=NST)
    CAST = sbt("tb_CAST", (128, 4160), BF16)
    KROW = sbt("tb_KROW", (128, 128), F32)

    prev_dma = None
    for pr in range(4):
        dc0 = 2 * pr
        ws = [cdone] + ([prev_dma] if prev_dma else [])
        for k in (0, 1):
            p.mset(pr3[:, k, 0, :], 1.0, waits=ws)
            p.mset(pi3[:, k, 0, :], 0.0)
            p.cp(rr3[:, k, 0, :], lamr[:, dc0 + k, :])
            p.cp(ri3[:, k, 0, :], lami[:, dc0 + k, :])
        for j in range(6):
            n = 1 << j
            ps_r, ps_i = pr3[:, :, 0:n, :], pi3[:, :, 0:n, :]
            nn_r, nn_i = pr3[:, :, n:2 * n, :], pi3[:, :, n:2 * n, :]
            R_r, R_i = rr3[:, :, 0:n, :], ri3[:, :, 0:n, :]
            A_, B_ = sa3[:, :, 0:n, :], sb3[:, :, 0:n, :]
            p.tt(A_, ps_r, R_r, ALU.mult)
            p.tt(B_, ps_i, R_i, ALU.mult)
            p.tt(nn_r, A_, B_, ALU.subtract)
            p.tt(A_, ps_r, R_i, ALU.mult)
            p.tt(B_, ps_i, R_r, ALU.mult)
            p.tt(nn_i, A_, B_, ALU.add)
            p.tt(A_, R_r, R_r, ALU.mult)
            p.tt(B_, R_i, R_i, ALU.mult)
            p.stt(R_i, R_r, 2.0, R_i, ALU.mult, ALU.mult)
            p.tt(R_r, A_, B_, ALU.subtract)
            if j < 5:
                p.cp(rr3[:, :, n:2 * n, :], R_r)
                p.cp(ri3[:, :, n:2 * n, :], R_i)
        p.cp(pr3[:, :, 64, :], rr3[:, :, 0, :])
        p.cp(pi3[:, :, 64, :], ri3[:, :, 0, :])
        for k in (0, 1):
            p.cp(l64r[:, dc0 + k, :], rr3[:, k, 0, :])
            p.cp(l64i[:, dc0 + k, :], ri3[:, k, 0, :])
        for k in (0, 1):
            p.cp(cr3[:, k, 0, :], crp[:, dc0 + k, :])
            p.cp(ci3[:, k, 0, :], cip[:, dc0 + k, :])
        for j in range(6):
            n = 1 << j
            p.cp(cr3[:, :, n:2 * n, :], cr3[:, :, 0:n, :])
            p.cp(ci3[:, :, n:2 * n, :], ci3[:, :, 0:n, :])
        p.tt(rr3, cr3, pr3[:, :, 0:64, :], ALU.mult)
        p.tt(ri3, ci3, pi3[:, :, 0:64, :], ALU.mult)
        p.tt(rr3, rr3, ri3, ALU.subtract)
        rm = p.red(KROW[:].rearrange("p (c r) -> p c r", c=2), rr3, op=ALU.add)
        p.cp(CAST[:, 0:128], KROW[:], waits=[rm])
        vk = p.mark("vector")
        kd = p.dma(apc(gbufh, (dc0 * 128) * 128 + 63,
                       [(128, 128), (128 * 128, 2), (1, 64)]),
                   CAST[:, 0:128].rearrange("p (c r) -> p c r", c=2), waits=[vk])
        p.cp(CAST[:, 0:4096], POW[:, 0:4160].rearrange(
            "p (c u n) -> p c u n", c=2, n=NST)[:, :, 0:64, :], waits=[kd])
        vc1 = p.mark("vector")
        vd1 = None
        for k in range(2):
            vd1 = p.dma(apc(vbufh, ((dc0 + k) * 128) * (CC * 64),
                            [(CC * 64, 128), (64, CC), (1, NST)]),
                        CAST[:, k * 2048:(k + 1) * 2048].rearrange(
                            "p (u n) -> p u n", n=NST), waits=[vc1])
        p.cp(CAST[:, 0:4096], POW[:, 4160:8320].rearrange(
            "p (c u n) -> p c u n", c=2, n=NST)[:, :, 0:64, :], waits=[vd1])
        vc2 = p.mark("vector")
        vd2 = None
        for k in range(2):
            vd2 = p.dma(apc(vbufh, ((dc0 + k) * 128) * (CC * 64) + NST,
                            [(CC * 64, 128), (64, CC), (1, NST)]),
                        CAST[:, k * 2048:(k + 1) * 2048].rearrange(
                            "p (u n) -> p u n", n=NST), waits=[vc2])
        shf_r = pr3[:, :, 1:65, :]
        shf_i = pi3[:, :, 1:65, :]
        p.tt(rr3, cr3, shf_r, ALU.mult, waits=[vd2])
        p.tt(ri3, ci3, shf_i, ALU.mult)
        p.tt(rr3, rr3, ri3, ALU.subtract)
        p.cp(CAST[:, 0:4096].rearrange("p (c n r) -> p c r n", c=2, n=NST),
             REP[:, 0:4096].rearrange("p (c r n) -> p c r n", c=2, n=NST))
        ve1 = p.mark("vector")
        ed1 = None
        for k in range(2):
            ed1 = p.dma(apc(ebufh, ((dc0 + k) * 128) * (2 * NST * CC),
                            [(2 * NST * CC, 128), (CC, NST), (1, CC)]),
                        CAST[:, k * 2048:(k + 1) * 2048].rearrange(
                            "p (n r) -> p n r", n=NST), waits=[ve1])
        p.tt(rr3, cr3, shf_i, ALU.mult, waits=[ed1])
        p.tt(ri3, ci3, shf_r, ALU.mult)
        p.tt(rr3, rr3, ri3, ALU.add)
        p.ts(REP[:, 0:4096], REP[:, 0:4096], -1.0, ALU.mult)
        p.cp(CAST[:, 0:4096].rearrange("p (c n r) -> p c r n", c=2, n=NST),
             REP[:, 0:4096].rearrange("p (c r n) -> p c r n", c=2, n=NST))
        ve2 = p.mark("vector")
        ed2 = None
        for k in range(2):
            ed2 = p.dma(apc(ebufh, ((dc0 + k) * 128) * (2 * NST * CC) + NST * CC,
                            [(2 * NST * CC, 128), (CC, NST), (1, CC)]),
                        CAST[:, k * 2048:(k + 1) * 2048].rearrange(
                            "p (n r) -> p n r", n=NST), waits=[ve2])
        prev_dma = ed2

    ZT = sbt("tb_ZT", (128, 64), BF16)
    zm = p.mset(ZT[:], 0.0, inc=1)
    for dc in range(8):
        p.dma(apc(gbufh, (dc * 128) * 128, [(128, 128), (1, 63)]), ZT[:, 0:63],
              waits=[zm])
        p.dma(apc(gbufh, (dc * 128) * 128 + 127, [(128, 128), (1, 1)]), ZT[:, 0:1],
              waits=[zm], allow_slow_non_contiguous=True)

    pL = pst("tb_pL", (32, 128), F32)
    idf = cst["identf"]
    vnow = p.mark("vector")
    lastcp = vnow
    for dc in range(8):
        g, sub = dc // 2, dc % 2
        for comp, dstT in ((0, Lr_t), (1, Li_t)):
            base = 1800 if comp == 0 else 2056
            tm = p.tr(pL[:], prm[:, base + dc * NST: base + (dc + 1) * NST],
                      idf, waits=[vnow, lastcp], inc=1)
            for rho in (0, 1):
                lastcp = p.cp(dstT[g][rho * 64 + sub * 32: rho * 64 + sub * 32 + 32,
                              :], pL[:], waits=[tm], inc=1)
    if globals().get("_TAP_PRM") is not None:
        p.dma(globals()["_TAP_PRM"], prm[:])
    done = p.barrier()
    es.close()
    return done



def build_program(n_layers=DEPTH, tap=None, stages=(1, 2, 3)):
    stack = contextlib.ExitStack()
    nc = bass.Bass("TRN2", target_bir_lowering=False)

    dr = {}
    def din(name, shape, dt=F32):
        dr[name] = nc.dram_tensor(name, list(shape), dt, kind="ExternalInput").ap()
        return dr[name]

    din("x", (L, D))
    din("log_dt", (D,))
    din("log_A_real", (D, NST))
    din("A_imag", (D, NST))
    din("C_real", (D, NST))
    din("C_imag", (D, NST))
    din("Wssm", (D, D))
    din("bssm", (D,))
    din("Wqkv", (DEPTH, D, DQKV))
    din("q_scale", (DEPTH, DH))
    din("k_scale", (DEPTH, DH))
    din("Wo", (DEPTH, D, D))
    din("ln_g", (DEPTH, D))
    din("ln_b", (DEPTH, D))
    din("W1", (DEPTH, D, D1))
    din("b1", (DEPTH, D1))
    din("W2", (DEPTH, DI, D))
    din("b2", (DEPTH, D))
    consts = host_constants()
    for k, v in consts.items():
        din(k, v.shape, BF16 if v.dtype == BFD else F32)

    out = nc.dram_tensor("out", [L, D], F32, kind="ExternalOutput").ap()
    tap_ap = None
    g = globals()
    g.setdefault("_TAP_PRM", None)
    if tap is not None:
        tshape = {"x1": (L, D), "yb": (D, L), "z0": (D, L), "z": (D, L),
                  "gbuf": (D, 128), "vbuf": (D, 128), "ebuf": (D, 128),
                  "qf": (128, 4096), "kw": (128, 4096), "ao": (128, 4096),
                  "vw": (128, 4096), "pb": (128, 6144), "ao0": (128, 4096), "xn": (128, 4096), "gg": (128, 22 * 512),
                  "prm": (128, 4360),
                  "ub": (128, 32 * D),
                  "sstb": (128, CC * 128), "pstt": (128, CC * 128)}[tap]
        tap_ap = nc.dram_tensor("tap", list(tshape), F32, kind="ExternalOutput").ap()

    zb = [nc.dram_tensor(f"zbuf{i}", [D, GW], F32, kind="Internal").ap()
          for i in range(2)]
    x1h = nc.dram_tensor("x1h", [L, D], F32, kind="Internal").ap()
    ybh = nc.dram_tensor("ybh", [D, L], BF16, kind="Internal").ap()
    gbufh = nc.dram_tensor("gbufh", [D, 128], BF16, kind="Internal").ap()
    vbufh = nc.dram_tensor("vbufh", [D, CC, 2 * NST], BF16, kind="Internal").ap()
    ebufh = nc.dram_tensor("ebufh", [D, 2 * NST, CC], BF16, kind="Internal").ap()
    w2s = nc.dram_tensor("w2s", [DI, D], BF16, kind="Internal").ap()

    p = Prog(nc, stack)
    g.setdefault("_TAP_PSTT", None)
    g.setdefault("_TAP_UB", None)
    g.setdefault("_TAP_SSTB", None)
    g.setdefault("_TAP_Z", None)
    g.setdefault("_TAP_A", None)
    g.setdefault("_TAP_AAP", None)
    if tap == "prm":
        g["_TAP_PRM"] = tap_ap
    if tap == "pstt":
        g["_TAP_PSTT"] = tap_ap
    if tap == "ub":
        g["_TAP_UB"] = tap_ap
    if tap == "sstb":
        g["_TAP_SSTB"] = tap_ap
    if tap in ("qf", "kw", "ao", "vw", "pb", "ao0", "xn", "gg"):
        g["_TAP_A"] = tap
        g["_TAP_AAP"] = tap_ap

    cst = {}
    for k, v in consts.items():
        if k in ("csq", "snq", "csk", "snk") or k.startswith(("bd16_", "bcp_")):
            continue
        dt = BF16 if v.dtype == BFD else F32
        cst[k] = stack.enter_context(nc.sbuf_tensor("c_" + k, list(v.shape), dt))[:]
        p.dma(cst[k], dr[k])

    es_s0 = contextlib.ExitStack()
    Lr_t = [es_s0.enter_context(nc.sbuf_tensor(f"Lr{g}", [128, 128], F32))[:]
            for g in range(4)]
    Li_t = [es_s0.enter_context(nc.sbuf_tensor(f"Li{g}", [128, 128], F32))[:]
            for g in range(4)]

    if 1 in stages:
        build_tables(nc, p, stack, dr, gbufh, vbufh, ebufh, Lr_t, Li_t, cst)
    if 2 in stages:
        build_conv(nc, p, stack, dr, cst, gbufh, vbufh, ebufh, Lr_t, Li_t, ybh)
    es_s0.close()
    if 3 in stages:
        build_proj(nc, p, stack, dr, cst, ybh, x1h, zb, tap, tap_ap)

    if n_layers > 0:
        build_layers(nc, p, stack, dr, cst, zb, n_layers, tap, tap_ap, w2s)

    if tap in ("gbuf",):
        with nc.sbuf_tensor("dbg_b", [128, 1024], BF16) as DB, \
             nc.sbuf_tensor("dbg_f", [128, 1024], F32) as DF:
            bar = p.barrier()
            pm = None
            for blk in range(8):
                d1 = p.dma(DB[:, 0:128], apc(gbufh, blk * 128 * 128,
                           [(128, 128), (1, 128)]),
                           waits=[bar[0]] if pm is None else [pm])
                cm = p.cp(DF[:, 0:128], DB[:, 0:128], waits=[d1], inc=1)
                pm = p.dma(apc(tap_ap, blk * 128 * 128,
                               [(128, 128), (1, 128)]), DF[:, 0:128], waits=[cm])
    build_final(nc, p, stack, dr, cst, zb, x1h, out, n_layers)

    blk = stack.enter_context(nc.Block())
    for eng in ENGS:
        sec = getattr(blk, eng)

        def make(stream, eng):
            sems = p.sem

            def _(h):
                for rec in stream:
                    for (we, wv) in rec["wl"]:
                        h.wait_ge(sems[we], wv)
                    call = rec["fn"](h)
                    if rec["inc"]:
                        call.then_inc(sems[eng], rec["inc"])
            return _
        sec(make(p.stream[eng], eng))
    stack.close()
    return nc




def build_conv(nc, p, stack, dr, cst, gbufh, vbufh, ebufh, Lr_t, Li_t, ybh):
    es = contextlib.ExitStack()
    sbt = lambda n, s, d: es.enter_context(nc.sbuf_tensor(n, list(s), d))
    pst = lambda n, s, d: es.enter_context(nc.psum_tensor(n, list(s), d))

    UB = sbt("UB", (128, 32 * D), BF16)
    es_ln = contextlib.ExitStack()
    XL = es_ln.enter_context(nc.sbuf_tensor("XL", [128, 2 * D], F32))
    STAT = es_ln.enter_context(nc.sbuf_tensor("STAT", [128, 24], F32))
    SCR0 = es_ln.enter_context(nc.sbuf_tensor("SCR0", [128, D], BF16))

    xin = dr["x"]
    prev = None
    for tt in range(32):
        buf = XL[:, (tt % 2) * D:(tt % 2) * D + D]
        src = apc(xin, (tt * 128 + 63) * D, [(64 * D, 2), (-D, 64), (1, D)])
        dm = p.dma(buf, src, waits=[prev] if prev else ())
        sx, sxx = STAT[:, 0:1], STAT[:, 1:2]
        mu, rs, ep = STAT[:, 2:3], STAT[:, 3:4], STAT[:, 4:5]
        m1 = p.ts(SCR0[:, 0:D], buf, 1.0, ALU.mult, s2=None, op1=ALU.add,
                  accum_out=sx, waits=[dm], inc=1)
        m2 = p.act(SCR0[:, 0:D], buf, AF.Square, accum_out=sxx, waits=[dm], inc=1)
        var_ = STAT[:, 5:6]
        lns = STAT[:, 8:16]
        lns2 = STAT[:, 16:24]
        p.ts(mu, sx, 1.0 / D, ALU.mult, waits=[m1, m2])
        w0 = p.ts(ep, sxx, 1.0 / D, ALU.mult, inc=1)
        p.stt(var_, mu, mu, ep, ALU.mult, ALU.subtract, waits=[w0], inc=1)
        w1 = p.mark("vector")
        for _k in range(8):
            p.ts(STAT[:, 8 + _k:9 + _k], var_, -1.0, ALU.mult, s2=1e-5,
                 op1=ALU.add, waits=[w1])
        vr = p.mark("vector")
        a1 = p.act(lns, lns, AF.Ln, waits=[vr], inc=1)
        a2 = p.act(lns2, lns, AF.Exp, scale=-0.5, waits=[a1], inc=1)
        rs = STAT[:, 16:17]
        prev = p.ts(UB[:, tt * D:(tt + 1) * D], buf, mu, ALU.subtract, s2=rs,
                    op1=ALU.mult, waits=[a2], inc=1)
    if globals().get("_TAP_UB") is not None:
        tdm = None
        for ck in range(16):
            c1 = p.cp(XL[:, 0:2048], UB[:, ck * 2048:(ck + 1) * 2048], inc=1,
                      waits=[tdm] if tdm else ())
            tdm = p.dma(globals()["_TAP_UB"][:, ck * 2048:(ck + 1) * 2048], XL[:, 0:2048],
                        waits=[c1])
    u_done = p.barrier()[0]
    es_ln.close()

    HS = [sbt(f"HS{i}", (128, 64 * 128), BF16) for i in range(2)]
    ES2 = [sbt(f"ES2{i}", (128, 64 * 64), BF16) for i in range(2)]
    PSTT = sbt("PSTT", (128, CC * 128), F32)
    SSTB = sbt("SSTB", (128, CC * 128), BF16)
    SRUN = sbt("SRUN", (128, 3 * 128), F32)
    YSTA = sbt("YSTA", (128, 256 * CC), BF16)
    YST2 = YSTA
    YBS = sbt("YBS", (128, 16 * CC), BF16)
    pbank = [pst(f"cvps{i}", (128, 512), F32) for i in range(8)]
    ptr = pbank[0][:].bitcast(BF16)

    identb = cst["identb"]
    ub_e = UB[0:64, :].rearrange("p (tt d) -> p tt d", d=D)
    ub_o = UB[64:128, :].rearrange("p (tt d) -> p tt d", d=D)
    ysta3 = YSTA[0:64, :].rearrange("p (ch s) -> p ch s", s=CC)
    yst23 = YST2[0:64, :].rearrange("p (ch s) -> p ch s", s=CC)

    p.mset(YSTA[64:128, :], 0.0, waits=[u_done])
    bank_free = [None] * 8
    hs_free = [None, None]
    es_free = [None, None]
    ph3_prev = None
    ph3_dma = None

    for g in range(CONV_GROUPS):
        pevac = []
        yevac = []
        for wave in range(4 if (CONV_PH & 1) else 0):
            ch0 = wave * 64
            dglob0 = g * 256 + ch0
            sub = ch0 // 128
            hs = HS[wave % 2]
            wlist = [u_done]
            if hs_free[wave % 2] is not None:
                wlist.append(hs_free[wave % 2])
            hs3 = hs[:, :].rearrange("p (c x) -> p c x", x=128)
            for half in range(2):
                p.dma(hs3[64 * half:64 * half + 64, :, 0:64],
                      apc(gbufh, dglob0 * 128, [(1, 64), (128, 64), (1, 64)]),
                      waits=wlist)
                p.dma(hs3[64 * half:64 * half + 64, :, 64:128],
                      apc(vbufh, dglob0 * CC * 64,
                          [(64, 64), (CC * 64, 64), (1, 64)]))
            ld = p.mark("sync")
            mm_marks = [None] * 8
            for c in range(64):
                ch = ch0 + c
                dglob = g * 256 + ch
                b = c % 4
                blk = c // 4
                w = [ld]
                for bb in (b, b + 4):
                    if bank_free[bb] is not None:
                        w.append(bank_free[bb])
                p.mm(pbank[b][:, blk * 32:blk * 32 + 32],
                     hs3[0:64, c, :], ub_e[:, :, dglob], True, True, waits=w,
                     tile_position=(0, 0))
                mm_marks[c % 4] = p.mm(
                    pbank[4 + b][:, blk * 32:blk * 32 + 32],
                    hs3[64:128, c, :], ub_o[:, :, dglob], True, True, inc=1,
                    tile_position=(64, 0))
            pst3 = PSTT[:, :].rearrange("p (s q) -> p s q", q=128)
            for b in range(4):
                q0 = (ch0 % 128) + b
                for par in range(2):
                    bank = pbank[b + 4 * par]
                    ye = p.cp(ysta3[:, ch0 + b:ch0 + b + 61:4,
                                    par * 32:par * 32 + 32],
                              bank[0:64, 0:512].rearrange(
                                  "p (k s) -> p k s", s=32),
                              waits=[mm_marks[b]], inc=1, eng="scalar")
                    yevac.append(ye)
                    for rho in range(2):
                        src = bank[64 + rho * 32:96 + rho * 32, 0:512].rearrange(
                            "p (k c2) -> p k c2", c2=32)
                        dst = pst3[rho * 64 + sub * 32: rho * 64 + sub * 32 + 32,
                                   par::2, q0:q0 + 61:4].transpose([0, 2, 1])
                        pe = p.cp(dst, src, waits=[mm_marks[b]], inc=1)
                        pevac.append(pe)
                    bank_free[b + 4 * par] = pevac[-1]
            hs_free[wave % 2] = p.mark("tensor")
        ph1v = p.mark("vector")
        ph1s = p.mark("scalar")
        if g == 0 and globals().get("_TAP_PSTT") is not None:
            p.dma(globals()["_TAP_PSTT"], PSTT[:], waits=[ph1v])

        S = SRUN[:, 0:128]
        T = SRUN[:, 128:256]
        U = SRUN[:, 256:384]
        sc = p.mset(S, 0.0, waits=[ph1v, ph1s], inc=1)
        do_scan = bool(CONV_PH & 2)
        Lr, Li = Lr_t[g], Li_t[g]
        pst2 = PSTT[:, :].rearrange("p (s q) -> p s q", q=128)
        sst2 = SSTB[:, :].rearrange("p (s q) -> p s q", q=128)
        sw = sc
        for s in range(CC if (CONV_PH & 2) else 0):
            p.cp(sst2[:, s, :], S, waits=[sw])
            p.tt(T, S, Lr, ALU.mult)
            p.tt(U[0:64, :], S[64:128, :], Li[64:128, :], ALU.mult)
            w2 = p.tt(U[64:128, :], S[0:64, :], Li[0:64, :], ALU.mult, inc=1)
            p.tt(S[0:64, :], T[0:64, :], U[0:64, :], ALU.subtract, waits=[w2])
            w3 = p.tt(S[64:128, :], T[64:128, :], U[64:128, :], ALU.add, inc=1)
            sw = p.tt(S, S, pst2[:, s, :], ALU.add, waits=[w3], inc=1)
        if g == 0 and globals().get("_TAP_SSTB") is not None:
            tdm2 = None
            for ck in range(32):
                cc1 = p.cp(SRUN[:, 0:256], SSTB[:, ck * 256:(ck + 1) * 256],
                           inc=1, waits=[tdm2] if tdm2 else ())
                tdm2 = p.dma(globals()["_TAP_SSTB"][:, ck * 256:(ck + 1) * 256],
                             SRUN[:, 0:256], waits=[cc1])
        scan_done = p.mark("vector")

        for wave in range(4 if (CONV_PH & 4) else 0):
            ch0 = wave * 64
            dglob0 = g * 256 + ch0
            sub = ch0 // 128
            esb = ES2[wave % 2]
            wlist = [scan_done]
            if es_free[wave % 2] is not None:
                wlist.append(es_free[wave % 2])
            ms = p.mset(esb[:], 0.0, waits=wlist, inc=1)
            es3 = esb[:, :].rearrange("p (c r) -> p c r", r=64)
            e1 = p.dma(es3[sub * 32:sub * 32 + 32, :, :],
                       apc(ebufh, dglob0 * 2 * NST * CC,
                           [(CC, 32), (2 * NST * CC, 64), (1, 64)]), waits=[ms])
            e2 = p.dma(es3[64 + sub * 32:64 + sub * 32 + 32, :, :],
                       apc(ebufh, dglob0 * 2 * NST * CC + NST * CC,
                           [(CC, 32), (2 * NST * CC, 64), (1, 64)]))
            ld = p.mark("sync")
            mm_marks = [None] * 8
            for c in range(64):
                ch = ch0 + c
                q = (ch0 % 128) + c
                b = c % 8
                blk = c // 8
                bank = pbank[b]
                w = [ld]
                if bank_free[b] is not None:
                    w.append(bank_free[b])
                out = bank[0:64, blk * 64:blk * 64 + 64]
                mv2 = SSTB[:, :].rearrange(
                    "p (c2 par q2) -> p par c2 q2", par=2, q2=128)[:, :, :, q]
                p.mm(out, es3[:, c, :], mv2, True, False, waits=w)
                mm_marks[b] = p.mm(out, identb[:, 0:64],
                                   YSTA[:, ch * 64:(ch + 1) * 64], False, True,
                                   inc=1, waits=[ph1s])
            for b in range(8):
                bank = pbank[b]
                ge = p.act(yst23[:, ch0 + b:ch0 + b + 57:8, :],
                           bank[0:64, 0:512].rearrange("p (k s) -> p k s", s=64),
                           AF.Gelu, waits=[mm_marks[b]], inc=1)
                bank_free[b] = ge
            es_free[wave % 2] = p.mark("tensor")
        ph2s = p.mark("scalar")

        for sub in range(2 if (CONV_PH & 8) else 0):
            for sblk in range(4):
                trm = None
                for si in range(16):
                    colpos = sblk * 16 + si
                    src = yst23[:, sub * 128:(sub + 1) * 128, colpos]
                    w = [ph2s] + ([ph3_prev] if ph3_prev else [])
                    if bank_free[0] is not None:
                        w.append(bank_free[0])
                    trm = p.tr(ptr[:, si * 64:(si + 1) * 64], src,
                               identb[0:64, 0:64], waits=w,
                               inc=(1 if si == 15 else 0))
                ev = p.cp(YBS[:], ptr[:],
                          waits=[trm] + ([ph3_dma] if ph3_dma else []), inc=1)
                tok0 = (sblk % 2) * 16 * 128 + (64 if sblk >= 2 else 0)
                ph3_dma = p.dma(
                    apc(ybh, (g * 256 + sub * 128) * L + tok0,
                        [(L, 128), (128, 16), (1, CC)]),
                    YBS[:, :].rearrange("p (si r) -> p si r", r=CC),
                    waits=[ev])
                ph3_prev = p.mark("sync")
                bank_free[0] = ev
        vb = p.mark("vector")
        for b in range(8):
            bank_free[b] = vb

    done = p.barrier()
    es.close()
    return done



def build_proj(nc, p, stack, dr, cst, ybh, x1h, zb, tap, tap_ap):
    es = contextlib.ExitStack()
    sbt = lambda n, s, d: es.enter_context(nc.sbuf_tensor(n, list(s), d))
    pst = lambda n, s, d: es.enter_context(nc.psum_tensor(n, list(s), d))

    identf = cst["identf"]
    identb = cst["identb"]
    ones128 = cst["ones128"]

    WL = sbt("WL", (128, D), F32)
    WBC = sbt("WBC", (128, D), BF16)
    WT = sbt("WT", (128, 8 * D), BF16)
    ptw = pst("ptw", (128, 512), BF16)
    brow = sbt("brow", (1, D), BF16)
    brf = sbt("brf", (1, D), F32)
    bm = p.dma(brf[:], dr["bssm"].rearrange("(a d) -> a d", a=1))
    p.cp(brow[:], brf[:], waits=[bm])
    wt3 = WT[:, :].rearrange("p (dc o) -> p dc o", o=D)
    wtm = None
    for oc in range(8):
        dm = p.dma(WL[:], dr["Wssm"][oc * 128:(oc + 1) * 128, :],
                   waits=[wtm] if wtm else ())
        cm = p.cp(WBC[:], WL[:], waits=[dm], inc=1)
        for grp in range(2):
            tms = None
            for k in range(4):
                dc = grp * 4 + k
                w = [cm] + ([wtm] if wtm else [])
                tms = p.tr(ptw[:, k * 128:(k + 1) * 128],
                           WBC[:, dc * 128:(dc + 1) * 128], identb,
                           waits=w, inc=(1 if k == 3 else 0))
            wtm = p.cp(wt3[:, grp * 4:grp * 4 + 4, oc * 128:(oc + 1) * 128],
                       ptw[:].rearrange("p (k o) -> p k o", o=128),
                       waits=[tms], inc=1)
    wt_done = p.barrier()[0]

    ZG = sbt("ZG", (128, 257), F32)
    zg = p.mset(ZG[:], 0.0, inc=1)
    for i in range(2):
        for dc in range(8):
            p.dma(apc(zb[i], dc * 128 * GW, [(GW, 128), (1, 257)]), ZG[:],
                  waits=[zg])

    YBT = sbt("YBT", (128, 8 * 128), BF16)
    XT = sbt("XT", (128, D), F32)
    X1T = sbt("X1T", (128, D), F32)
    VTf = sbt("VTf", (128, D), F32)
    Z0S = sbt("Z0S", (128, 8 * 128), F32)
    STT2 = sbt("STT2", (128, 24), F32)
    SC2 = sbt("SC2", (128, D), BF16)
    pa = [pst(f"pa{i}", (128, 512), F32) for i in range(2)]
    pz = [pst(f"pz{i}", (128, 512), F32) for i in range(2)]

    prevv = None
    x1m_prev = None
    z0d_prev = None
    for tc in range(32):
        wl = [wt_done] + ([prevv] if prevv else [])
        ybm = p.dma(YBT[:], apc(ybh, tc * 128, [(L, 128), (128 * L, 8), (1, 128)]),
                    waits=wl)
        xtm = p.dma(XT[:], dr["x"][tc * 128:(tc + 1) * 128, :])
        ld = p.mark("sync")
        mmk = []
        for half in range(2):
            for dc in range(8):
                p.mm(pa[half][:], YBT[:, dc * 128:(dc + 1) * 128],
                     WT[:, dc * D + half * 512: dc * D + half * 512 + 512],
                     dc == 0, False, waits=[ld] + ([prevv] if prevv else []))
            mmk.append(p.mm(pa[half][:], ones128[0:1, 0:128],
                            brow[:, half * 512:(half + 1) * 512], False, True,
                            inc=1))
        sx, sxx = STT2[:, 0:1], STT2[:, 1:2]
        mu, rs, ep = STT2[:, 2:3], STT2[:, 3:4], STT2[:, 4:5]
        var_ = STT2[:, 5:6]
        h1 = p.tt(X1T[:, 0:512], pa[0][:], XT[:, 0:512], ALU.add,
                  waits=[mmk[0], xtm] + ([x1m_prev] if x1m_prev else []))
        h2 = p.tt(X1T[:, 512:1024], pa[1][:], XT[:, 512:1024], ALU.add,
                  waits=[mmk[1]], inc=1)
        x1m = p.dma(x1h[tc * 128:(tc + 1) * 128, :], X1T[:], waits=[h2])
        x1m_prev = x1m
        m1 = p.ts(SC2[:, 0:D], X1T[:], 1.0, ALU.mult, s2=None, op1=ALU.add,
                  accum_out=sx, inc=1)
        m2 = p.act(SC2[:, 0:D], X1T[:], AF.Square, accum_out=sxx, waits=[h2], inc=1)
        lns = STT2[:, 8:16]
        lns2 = STT2[:, 16:24]
        p.ts(mu, sx, 1.0 / D, ALU.mult, waits=[m1, m2])
        w0 = p.ts(ep, sxx, 1.0 / D, ALU.mult, inc=1)
        p.stt(var_, mu, mu, ep, ALU.mult, ALU.subtract, waits=[w0], inc=1)
        w1 = p.mark("vector")
        for _k in range(8):
            p.ts(STT2[:, 8 + _k:9 + _k], var_, -1.0, ALU.mult, s2=1e-5,
                 op1=ALU.add, waits=[w1])
        vr = p.mark("vector")
        a1 = p.act(lns, lns, AF.Ln, waits=[vr], inc=1)
        a2 = p.act(lns2, lns, AF.Exp, scale=-0.5, waits=[a1], inc=1)
        rs = STT2[:, 16:17]
        vm = p.ts(VTf[:], X1T[:], mu, ALU.subtract, s2=rs, op1=ALU.mult,
                  waits=[a2], inc=1)
        evs = []
        for grp in range(2):
            bsel = pz[grp]
            tms = None
            for k in range(4):
                dc = grp * 4 + k
                w = [vm] + (evs[-2:] if evs else [])
                tms = p.tr(bsel[:, k * 128:(k + 1) * 128],
                           VTf[:, dc * 128:(dc + 1) * 128], identf,
                           waits=w, inc=(1 if k == 3 else 0))
            evs.append(p.cp(Z0S[:, grp * 512:(grp + 1) * 512], bsel[:],
                            waits=[tms] + ([z0d_prev] if z0d_prev else []),
                            inc=1))
        prevv = evs[-1]
        z0d_prev = p.dma(apc(zb[0], 257 + tc * 128,
                             [(GW, 128), (128 * GW, 8), (1, 128)]),
                         Z0S[:].rearrange("p (dc t) -> p dc t", t=128),
                         waits=[prevv])
        if tap == "x1":
            p.dma(tap_ap[tc * 128:(tc + 1) * 128, :], X1T[:], waits=[h2])
        if tap == "yb":
            cm2 = p.cp(VTf[:], YBT[:].rearrange("p (dc t) -> p dc t", t=128),
                       waits=[ybm], inc=1)
            p.dma(apc(tap_ap, tc * 128, [(L, 128), (128 * L, 8), (1, 128)]),
                  VTf[:].rearrange("p (dc t) -> p dc t", t=128), waits=[cm2])
        if tap == "z0":
            p.dma(apc(tap_ap, tc * 128, [(L, 128), (128 * L, 8), (1, 128)]),
                  Z0S[:].rearrange("p (dc t) -> p dc t", t=128), waits=[prevv])
    done = p.barrier()
    es.close()




def build_layers(nc, p, stack, dr, cst, zb, n_layers, tap, tap_ap, w2s):
    es = contextlib.ExitStack()
    sbt = lambda n, s, d: es.enter_context(nc.sbuf_tensor(n, list(s), d))
    pst = lambda n, s, d: es.enter_context(nc.psum_tensor(n, list(s), d))

    identb = cst["identb"]
    maskt = cst["maskt"]
    permq = cst["permq"]
    ones128 = cst["ones128"]

    fence = p.barrier()
    for ch in range(8):
        cst[f"bd16_{ch}"] = sbt(f"c2_bd16_{ch}", (128, 16), BF16)[:]
        p.dma(cst[f"bd16_{ch}"], dr[f"bd16_{ch}"])
        cst[f"bcp_{ch}"] = sbt(f"c2_bcp_{ch}", (16, 128), BF16)[:]
        p.dma(cst[f"bcp_{ch}"], dr[f"bcp_{ch}"])
    LN8V = sbt("LN8V", (128, 8), F32)
    ln8m = p.mset(LN8V[:], LN8, inc=1)
    pm_cst2 = p.mark("sync")

    ZL = sbt("ZL", (128, 8 * 769), F32)
    XB = sbt("XB", (128, 8 * 768), BF16)
    Z1T = sbt("Z1T", (128, 4 * 512), F32)
    VECS = sbt("VECS", (128, 72), F32)
    QB = sbt("QB", (128, 512), BF16)
    Q2 = sbt("Q2", (128, 512), BF16)
    T1 = sbt("T1", (128, 512), BF16)
    U1 = sbt("U1", (128, 512), BF16)

    ARENA = sbt("ARENA", (128, 74496), BF16)

    class _Carve:
        def __init__(self):
            self.off = 0

        def take(self, shape, dt):
            size = 2 if dt == F32 else 2 if dt == BF16 else 4
            esz = 4 if dt == F32 else 2
            self.off = (self.off + 3) & ~3
            off_el = self.off // esz
            v = ARENA[:].bitcast(dt) if dt == F32 else ARENA[:]
            out = v[0:shape[0], off_el:off_el + shape[1]]
            self.off += shape[1] * esz
            assert self.off <= 74496 * 2, f"arena overflow {self.off}"
            return out

    ca = _Carve()
    WQB = ca.take((128, 8 * DQKV), BF16)
    WOB = ca.take((128, 8 * D), BF16)
    KWIN = ca.take((128, 8 * 768), BF16)
    VWIN = ca.take((128, 6 * 1024), BF16)
    QF = ca.take((128, 8 * 512), BF16)
    KPRE = ca.take((128, 8 * 512), BF16)
    AO = ca.take((128, 8 * 512), BF16)
    PB = ca.take((128, 16 * 384), BF16)
    PTS = ca.take((128, 6 * 128), BF16)
    ROTT = ca.take((128, 4 * 512), BF16)
    RQS = ca.take((16, 512), F32)
    RQB = ca.take((16, 512), BF16)
    RKB = ca.take((16, 512), BF16)
    NMS = ca.take((128, 16), F32)
    DENS = ca.take((128, 16), F32)
    RDS = ca.take((128, 16), F32)

    cb = _Carve()
    W1B = cb.take((128, 8 * D1), BF16)
    W2BH0 = cb.take((128, 22 * 256), BF16)
    W2BH1 = cb.take((128, 22 * 256), BF16)
    GG = cb.take((128, 22 * 512), BF16)
    XN = cb.take((128, 8 * 512), BF16)
    SQT = cb.take((128, 512), BF16)
    MUB = cb.take((128, 512), F32)
    VARB = cb.take((128, 512), F32)
    RSB = cb.take((128, 512), F32)
    GE = cb.take((128, 512), BF16)

    pk = [pst(f"lps{i}", (128, 512), F32) for i in range(7)]
    ppt = pst("lppt", (128, 768), BF16)
    prn = pk[6][0:16, :]

    pm = {}

    qsv, ksv = VECS[:, 0:1], VECS[:, 1:2]
    qspv, kspv = VECS[:, 2:3], VECS[:, 3:4]

    for li in range(n_layers):
        cm = pm.get("wa_read")
        if cm is None and pm.get("zl_read"):
            cm = pm["zl_read"]
        if cm is None:
            cm = fence[0]
        WLDA = PB.bitcast(F32)
        LH = [WLDA[:, 0:1536], WLDA[:, 1536:3072]]
        cmv = [None, None]
        ci = 0
        for dc in range(8):
            for hf in range(2):
                sl = ci % 2
                w = [cmv[sl]] if cmv[sl] else [cm]
                ld = p.dma(LH[sl],
                           dr["Wqkv"][li, dc * 128:(dc + 1) * 128,
                                      hf * 1536:(hf + 1) * 1536],
                           eng="gpsimd", waits=w)
                cmv[sl] = p.cp(WQB[:, dc * DQKV + hf * 1536:
                                   dc * DQKV + (hf + 1) * 1536], LH[sl],
                               waits=[ld], inc=1,
                               eng="vector" if sl == 0 else "scalar")
                ci += 1
        for dc in range(8):
            sl = ci % 2
            ld = p.dma(LH[sl][:, 0:1024], dr["Wo"][li, dc * 128:(dc + 1) * 128, :],
                       eng="gpsimd", waits=[cmv[sl]])
            cmv[sl] = p.cp(WOB[:, dc * D:(dc + 1) * D], LH[sl][:, 0:1024],
                           waits=[ld], inc=1,
                           eng="vector" if sl == 0 else "scalar")
            ci += 1
        wa_cast = cmv[0]
        wa_cast2 = cmv[1]
        vw = [pm["v_read"]] if pm.get("v_read") else []
        p.dma(VECS[0:64, 0:1], apc(dr["q_scale"], li * DH, [(1, 64), (1, 1)]),
              allow_slow_non_contiguous=True, waits=vw)
        p.dma(VECS[64:128, 0:1], apc(dr["q_scale"], li * DH, [(1, 64), (1, 1)]),
              allow_slow_non_contiguous=True)
        p.dma(VECS[0:64, 1:2], apc(dr["k_scale"], li * DH, [(1, 64), (1, 1)]),
              allow_slow_non_contiguous=True)
        p.dma(VECS[64:128, 1:2], apc(dr["k_scale"], li * DH, [(1, 64), (1, 1)]),
              allow_slow_non_contiguous=True)
        for half in (0, 64):
            for (col, nm) in ((2, "q_scale"), (3, "k_scale")):
                p.dma(VECS[half:half + 32, col:col + 1],
                      apc(dr[nm], li * DH + 32, [(1, 32), (1, 1)]),
                      allow_slow_non_contiguous=True)
                p.dma(VECS[half + 32:half + 64, col:col + 1],
                      apc(dr[nm], li * DH, [(1, 32), (1, 1)]),
                      allow_slow_non_contiguous=True)
        p.dma(VECS[:, 4:12], apc(dr["ln_g"], li * D, [(1, 128), (128, 8)]),
              allow_slow_non_contiguous=True)
        p.dma(VECS[:, 12:20], apc(dr["ln_b"], li * D, [(1, 128), (128, 8)]),
              allow_slow_non_contiguous=True)
        p.dma(VECS[:, 20:28], apc(dr["b2"], li * D, [(1, 128), (128, 8)]),
              allow_slow_non_contiguous=True)
        for js in range(22):
            wcols = 128 if js < 21 else 42
            p.dma(VECS[0:wcols, 28 + js:29 + js],
                  apc(dr["b1"], li * D1 + js * 128, [(1, wcols), (1, 1)]),
                  allow_slow_non_contiguous=True)
            p.dma(VECS[0:wcols, 50 + js:51 + js],
                  apc(dr["b1"], li * D1 + DI + js * 128, [(1, wcols), (1, 1)]),
                  allow_slow_non_contiguous=True)
        wv_mark = p.mark("sync")

        zsrc, zdst = zb[0], zb[1]
        z1_dma = None
        for ti in range(NTILE):
            t0 = ti * TW
            zl3 = ZL[:, :].rearrange("p (dc c) -> p dc c", c=769)
            xb3 = XB[:, :].rearrange("p (dc c) -> p dc c", c=768)
            kw3 = KWIN.rearrange("p (h c) -> p h c", c=768)

            kroll = None
            if ti > 0:
                kroll = p.cp(kw3[:, :, 0:256], kw3[:, :, 512:768],
                             waits=[pm["qk_read"]], inc=1)
            rotw = [pm["rot_read"]] if pm.get("rot_read") else []
            rm0 = p.dma(ROTT[:, 0:512], dr["csq"][:, t0:t0 + 512], waits=rotw)
            p.dma(ROTT[:, 512:1024], dr["snq"][:, t0:t0 + 512])
            p.dma(ROTT[:, 1024:1536], dr["csk"][:, t0:t0 + 512])
            p.dma(ROTT[:, 1536:2048], dr["snk"][:, t0:t0 + 512])
            wl = [wa_cast, wa_cast2, wv_mark]
            if pm.get("zl_read"):
                wl.append(pm["zl_read"])
            if z1_dma:
                wl.append(z1_dma)
            rot_m = p.mark("sync")
            zl_m0 = p.dma(ZL[:, 0:4 * 769],
                          apc(zsrc, t0, [(GW, 128), (128 * GW, 4), (1, 769)]),
                          waits=wl)
            zl_m = p.dma(ZL[:, 4 * 769:8 * 769],
                         apc(zsrc, t0 + 4 * 128 * GW,
                             [(GW, 128), (128 * GW, 4), (1, 769)]))
            xb_m0 = p.cp(xb3[:, 0:4, :], zl3[:, 0:4, 1:769],
                         waits=[zl_m0, pm.get("xb_read")], inc=1)
            xb_m = p.cp(xb3[:, 4:8, :], zl3[:, 4:8, 0:768], waits=[zl_m],
                        inc=1)

            fold_marks = {}
            for kind in range(2):
                DST = QF if kind == 0 else KPRE
                sv = qsv if kind == 0 else ksv
                spv = qspv if kind == 0 else kspv
                CS = ROTT[:, 0:512] if kind == 0 else ROTT[:, 1024:1536]
                SN = ROTT[:, 512:1024] if kind == 0 else ROTT[:, 1536:2048]
                pre_marks = []
                for hp in range(8):
                    oc = kind * 8 + hp
                    bank = pk[oc % 2]
                    w = [xb_m0, rot_m]
                    for k2 in (f"qkb{oc % 2}", f"sc{oc % 2}"):
                        if pm.get(k2):
                            w.append(pm[k2])
                    for dc in range(8):
                        p.mm(bank[:], WQB[:, dc * DQKV + oc * 128:
                                          dc * DQKV + (oc + 1) * 128],
                             xb3[:, dc, 256:768], dc == 0, dc == 7,
                             waits=w if dc == 0 else
                             ([xb_m] if dc == 4 else ()))
                    mmq = p.mark("tensor")
                    e1 = p.cp(QB[:], bank[:], waits=[mmq, pm.get("qb_read")], inc=1)
                    e2 = p.tt(Q2[:], QB[:], QB[:], ALU.mult, waits=[e1], inc=1)
                    p.mm(prn, cst[f"bd16_{hp}"], Q2[:], hp == 0, hp == 7,
                         waits=[e2, pm_cst2] + ([pm[k2] for k2 in
                                                 ("rn_read", "avb0")
                                        if (hp == 0 and pm.get(k2))]))
                    pmm = p.mm(pk[2][:], permq, QB[:], True, True,
                               waits=[e1] + ([pm["perm_read"]] if
                                             pm.get("perm_read") else []), inc=1)
                    p.stt(T1[:], QB[:], sv, CS, ALU.mult, ALU.mult,
                          waits=[e1, pm.get("t1_read")])
                    u1m = p.stt(U1[:], pk[2][:], spv, SN, ALU.mult, ALU.mult,
                                waits=[pmm, pm.get("t1_read")], inc=1)
                    pm["perm_read"] = u1m
                    r3 = p.tt(DST[:, hp * 512:(hp + 1) * 512], T1[:], U1[:],
                              ALU.add, waits=[u1m] +
                              ([pm["qf_read"]] if (hp == 0 and kind == 0 and
                               pm.get("qf_read")) else []), inc=1)
                    pm["t1_read"] = r3
                    pm["qb_read"] = r3
                    pm[f"qkb{oc % 2}"] = r3
                    pre_marks.append(r3)
                rn_mm = p.mark("tensor")
                rv = p.ts(RQS, prn[:], 1e-24, ALU.add,
                          waits=[rn_mm, pm.get("rqs_read")], inc=1)
                a1 = p.act(RQS, RQS, AF.Ln, waits=[rv, ln8m], inc=1)
                RB = RQB if kind == 0 else RKB
                a2 = p.act(RB[:], RQS, AF.Exp, scale=-0.5,
                           bias=(LN8V[0:16, 0:1] if kind == 0 else 0.0),
                           waits=[a1, pm.get("rb_read")], inc=1)
                pm["rqs_read"] = a2
                pm["rn_read"] = a2
                for hp in range(8):
                    bm = p.mm(pk[3][:], cst[f"bcp_{hp}"], RB[:], True, True,
                              waits=[a2, pm_cst2] +
                              ([pm["bcb"]] if pm.get("bcb") else []), inc=1)
                    dst_sl = DST[:, hp * 512:(hp + 1) * 512]
                    fold = p.tt(dst_sl, dst_sl, pk[3][:], ALU.mult,
                                waits=[bm, pre_marks[hp]], inc=1)
                    pm["bcb"] = fold
                    if kind == 1:
                        p.cp(kw3[:, hp, 256:768], dst_sl,
                             waits=[fold] + ([kroll] if kroll else []), inc=1)
                pm["rb_read"] = p.mark("vector")
            qf_done = p.mark("vector")
            if li == 0 and ti == 0 and globals().get("_TAP_A") in ("qf", "kw"):
                src = QF if globals()["_TAP_A"] == "qf" else KWIN[:, 0:4096]
                bar = p.barrier()
                tdm = None
                for ck in range(8):
                    cc1 = p.cp(Z1T[:, 0:512], src[:, ck * 512:(ck + 1) * 512],
                               inc=1, waits=[bar[0]] + ([tdm] if tdm else []))
                    tdm = p.dma(globals()["_TAP_AAP"][:, ck * 512:(ck + 1) * 512],
                                Z1T[:, 0:512], waits=[cc1])
                p.barrier()

            vt_marks = []
            for tb in range(4):
                slot = (ti * 4 + tb + 2) % 6
                w = [xb_m0, qf_done]
                for key in ("qkb0", "qkb1", "sc2", "sc3", f"vwin_read_{slot}"):
                    if pm.get(key):
                        w.append(pm[key])
                for dc in range(8):
                    st = xb3[:, dc, 256 + tb * 128:256 + (tb + 1) * 128]
                    p.mm(pk[4][:], st, WQB[:, dc * DQKV + 2048:dc * DQKV + 2560],
                         dc == 0, dc == 7, waits=w if dc == 0 else
                         ([xb_m] if dc == 4 else ()))
                    p.mm(pk[5][:], st, WQB[:, dc * DQKV + 2560:dc * DQKV + 3072],
                         dc == 0, dc == 7)
                mmv = p.mark("tensor")
                p.cp(VWIN[:, slot * 1024:slot * 1024 + 512], pk[4][:],
                     waits=[mmv], inc=1, eng="scalar")
                v2 = p.cp(VWIN[:, slot * 1024 + 512:(slot + 1) * 1024], pk[5][:],
                          inc=1, eng="scalar")
                vt_marks.append(v2)
                pm["qkb0"] = v2
                pm["qkb1"] = v2
            vt_done = vt_marks[-1]

            ao_marks = []
            avm = None
            qkbanks = [pk[0], pk[1], pk[4], pk[5]]
            for qc in range(4):
                qcg = ti * 4 + qc
                nj = min(3, qcg + 1)
                wwin = nj * 128
                kcol0 = 256 if qcg <= 1 else qc * 128
                exp_marks = []
                for h in range(16):
                    hp, hh = h // 2, h % 2
                    base = 64 * hh
                    sb_ = qkbanks[h % 4]
                    w = [qf_done, vt_done]
                    if pm.get(f"sc{h % 4}"):
                        w.append(pm[f"sc{h % 4}"])
                    qst = QF[base:base + 64,
                             hp * 512 + qc * 128:hp * 512 + (qc + 1) * 128]
                    kmv = KWIN[base:base + 64,
                               hp * 768 + kcol0:hp * 768 + kcol0 + wwin]
                    p.mm(sb_[:, 0:wwin], qst, kmv, True, False, waits=w,
                         tile_position=(base, 0))
                    if qcg == 0:
                        p.mm(sb_[:, 0:128], identb, maskt[:, 128:256],
                             False, True)
                    else:
                        if nj == 3:
                            p.mm(sb_[:, 0:128], identb, maskt[:, 0:128],
                                 False, False)
                        p.mm(sb_[:, wwin - 128:wwin], identb,
                             maskt[:, 128:256], False, True)
                    msc = p.mark("tensor")
                    rdm = p.red(NMS[:, h:h + 1], sb_[:, 0:wwin], op=ALU.max,
                                negate=True, waits=[msc])
                    ex = p.act(PB[:, h * 384:h * 384 + wwin], sb_[:, 0:wwin],
                               AF.Exp, bias=NMS[:, h:h + 1],
                               accum_out=DENS[:, h:h + 1],
                               waits=[rdm, pm.get("pb_read")], inc=1)
                    exp_marks.append(ex)
                    pm[f"sc{h % 4}"] = ex
                rdv = p.op("vector",
                           lambda v: v.reciprocal(out=RDS, in_=DENS),
                           waits=exp_marks + ([pm["rds_read"]] if
                                              pm.get("rds_read") else []), inc=1)
                for hp in range(8):
                    trm = None
                    for hh in range(2):
                        h = 2 * hp + hh
                        nm2 = p.ts(PB[:, h * 384:h * 384 + wwin],
                                   PB[:, h * 384:h * 384 + wwin],
                                   RDS[:, h:h + 1], ALU.mult, waits=[rdv], inc=1)
                        for c in range(nj):
                            w2_ = [nm2]
                            if pm.get("ppt_read"):
                                w2_.append(pm["ppt_read"])
                            trm = p.tr(ppt[:, (hh * 3 + c) * 128:
                                           (hh * 3 + c) * 128 + 128],
                                       PB[:, h * 384 + c * 128:
                                          h * 384 + (c + 1) * 128],
                                       identb, waits=w2_, inc=1)
                    pte = p.cp(PTS[:, 0:768], ppt[:, 0:768],
                               waits=[trm] + ([pm["pts_read"]] if
                                              pm.get("pts_read") else []), inc=1)
                    pm["ppt_read"] = pte
                    avbank = pk[6] if hp < 4 else pk[2]
                    reg = (hp % 4) * 128
                    w = [pte]
                    if pm.get(f"avb{hp // 4}"):
                        w.append(pm[f"avb{hp // 4}"])
                    for c in range(nj):
                        jblk = qcg - (nj - 1) + c
                        slot = (jblk + 2) % 6
                        for hh in range(2):
                            h = 2 * hp + hh
                            p.mm(avbank[64 * hh:64 * hh + 64, reg:reg + 128],
                                 VWIN[:, slot * 1024 + h * 64:
                                      slot * 1024 + (h + 1) * 64],
                                 PTS[:, (hh * 3 + c) * 128:
                                     (hh * 3 + c) * 128 + 128],
                                 c == 0, False,
                                 waits=w if (c == 0 and hh == 0) else (),
                                 tile_position=(0, 64 * hh))
                    avm = p.mark("tensor")
                    pm["pts_read"] = avm
                    for c in range(nj):
                        pm[f"vwin_read_{(qcg - (nj - 1) + c + 2) % 6}"] = avm
                pm["rds_read"] = avm
                pm["pb_read"] = avm
                ev0 = p.op("scalar", lambda s2, qc=qc: s2.copy(
                    out=AO.rearrange("p (hp c) -> p hp c", c=512)
                    [:, 0:4, qc * 128:(qc + 1) * 128],
                    in_=pk[6][:].rearrange("p (r c) -> p r c", c=128)),
                    waits=[avm] + ([pm["ao_read"]] if pm.get("ao_read") else []),
                    inc=1)
                ev1 = p.op("scalar", lambda s2, qc=qc: s2.copy(
                    out=AO.rearrange("p (hp c) -> p hp c", c=512)
                    [:, 4:8, qc * 128:(qc + 1) * 128],
                    in_=pk[2][:].rearrange("p (r c) -> p r c", c=128)),
                    inc=1)
                pm["avb0"] = ev0
                pm["avb1"] = ev1
                ao_marks.append(ev1)
                if (li == 0 and ti == 0 and qc == 0 and
                        globals().get("_TAP_A") == "ao0"):
                    bar = p.barrier()
                    tdm = None
                    for ck in range(8):
                        cc1 = p.cp(Z1T[:, 0:512], AO[:, ck * 512:(ck + 1) * 512],
                                   inc=1, waits=[bar[0]] + ([tdm] if tdm else []))
                        tdm = p.dma(globals()["_TAP_AAP"][:, ck * 512:(ck + 1) * 512],
                                    Z1T[:, 0:512], waits=[cc1])
                    p.barrier()
            pm["qk_read"] = p.mark("tensor")
            pm["qf_read"] = pm["qk_read"]
            pm["rot_read"] = pm["qk_read"]
            pm["perm_read"] = pm["avb1"]
            if li == 0 and ti == 0 and globals().get("_TAP_A") == "vw":
                bar = p.barrier()
                tdm = None
                for ck in range(8):
                    cc1 = p.cp(Z1T[:, 0:512],
                               VWIN[:, 2048 + ck * 512:2048 + (ck + 1) * 512],
                               inc=1, waits=[bar[0]] + ([tdm] if tdm else []))
                    tdm = p.dma(globals()["_TAP_AAP"][:, ck * 512:(ck + 1) * 512],
                                Z1T[:, 0:512], waits=[cc1])
                p.barrier()
            if li == 0 and ti == 0 and globals().get("_TAP_A") == "ao":
                bar = p.barrier()
                tdm = None
                for ck in range(8):
                    cc1 = p.cp(Z1T[:, 0:512], AO[:, ck * 512:(ck + 1) * 512],
                               inc=1, waits=[bar[0]] + ([tdm] if tdm else []))
                    tdm = p.dma(globals()["_TAP_AAP"][:, ck * 512:(ck + 1) * 512],
                                Z1T[:, 0:512], waits=[cc1])
                p.barrier()

            z1m = []
            h0 = None
            for oc in range(8):
                bank = pk[oc % 2]
                w = ao_marks + [pm[k2] for k2 in (f"qkb{oc % 2}", f"sc{oc % 2}")
                                if pm.get(k2)]
                for dc in range(8):
                    p.mm(bank[:], WOB[:, dc * D + oc * 128:dc * D + (oc + 1) * 128],
                         AO[:, dc * 512:(dc + 1) * 512], dc == 0, dc == 7,
                         waits=w if dc == 0 else ())
                mwo = p.mark("tensor")
                zw = [mwo]
                if oc == 0 and z1_dma:
                    zw.append(z1_dma)
                if oc == 4:
                    zw.append(h0)
                zz = p.tt(Z1T[:, (oc % 4) * 512:(oc % 4 + 1) * 512], bank[:],
                          zl3[:, oc, 257:769], ALU.add, waits=zw, inc=1)
                pm[f"qkb{oc % 2}"] = zz
                z1m.append(zz)
                if oc == 3:
                    h0 = p.dma(apc(zdst, 257 + t0,
                                   [(GW, 128), (128 * GW, 4), (1, 512)]),
                               Z1T[:].rearrange("p (dc c) -> p dc c", c=512),
                               waits=z1m[0:4])
            tmk = p.mark("tensor")
            pm["xb_read"] = tmk
            pm["ao_read"] = tmk
            pm["zl_read"] = z1m[-1]
            pm["v_read"] = z1m[-1]
            z1_dma = p.dma(apc(zdst, 257 + t0 + 4 * 128 * GW,
                               [(GW, 128), (128 * GW, 4), (1, 512)]),
                           Z1T[:].rearrange("p (dc c) -> p dc c", c=512),
                           waits=z1m[4:8])
        pm["za_done"] = z1_dma

        if tap == "z" and globals().get("_TAP_Z") == (li, "a"):
            bar = p.barrier()
            p.dma(tap_ap, apc(zb[1], 257, [(GW, D), (1, L)]), waits=[bar[0]])

        WLDB = GG.bitcast(F32)
        LB = [WLDB[:, 0:2730], WLDB[:, 2730:5460]]
        bguard = [pm["xb_read"], pm["za_done"], pm["zl_read"]]
        cmb = [None, None]
        ci = 0
        for dc in range(8):
            for hf in range(2):
                sl = ci % 2
                w = [cmb[sl]] if cmb[sl] else bguard
                ld = p.dma(LB[sl][:, 0:2730],
                           dr["W1"][li, dc * 128:(dc + 1) * 128,
                                    hf * 2730:(hf + 1) * 2730],
                           eng="gpsimd", waits=w)
                cmb[sl] = p.cp(W1B[:, dc * D1 + hf * 2730:
                                   dc * D1 + (hf + 1) * 2730],
                               LB[sl][:, 0:2730], waits=[ld], inc=1,
                               eng="vector" if sl == 0 else "scalar")
                ci += 1
        wb_cast = cmb[0]
        wb_cast2 = cmb[1]
        stb = [None, None]
        for ks in range(22):
            rows = 128 if ks < 21 else DI - 21 * 128
            sl = ks % 2
            ld = p.dma(LB[sl][0:rows, 0:1024],
                       dr["W2"][li, ks * 128:ks * 128 + rows, :],
                       eng="gpsimd", waits=[cmb[sl]])
            WH2 = W2BH0 if sl == 0 else W2BH1
            cw = [ld]
            if stb[sl]:
                cw.append(stb[sl])
            else:
                cw += bguard + [pm[k] for k in (f"w2h_read{sl}",) if pm.get(k)]
            cmb[sl] = p.cp(WH2[0:rows, 0:1024], LB[sl][0:rows, 0:1024],
                           waits=cw, inc=1, eng="scalar")
            stb[sl] = p.dma(apc(w2s, ks * 128 * D, [(D, rows), (1, D)]),
                            WH2[0:rows, 0:1024], waits=[cmb[sl]], eng="gpsimd")
        wb_st = [stb[0], stb[1]]
        pm["gg_read"] = cmb[1]

        zsrc, zdst = zb[1], zb[0]
        z2_dma = None
        for ti in range(NTILE):
            t0 = ti * TW
            xn3 = XN.rearrange("p (dc c) -> p dc c", c=512)
            wl = [pm["za_done"], wb_cast, wb_cast2]
            if pm.get("zl_read"):
                wl.append(pm["zl_read"])
            if z2_dma:
                wl.append(z2_dma)
            zl_m0 = p.dma(ZL[:, 0:4 * 513], apc(zsrc, 256 + t0,
                          [(GW, 128), (128 * GW, 4), (1, 513)]), waits=wl)
            zl_m = p.dma(ZL[:, 4 * 513:8 * 513],
                         apc(zsrc, 256 + t0 + 4 * 128 * GW,
                             [(GW, 128), (128 * GW, 4), (1, 513)]))
            zlb = ZL[:, 0:8 * 513].rearrange("p (dc c) -> p dc c", c=513)
            xsb3 = XB[:, 0:8 * 512].rearrange("p (dc c) -> p dc c", c=512)
            cpa = p.cp(xsb3[:, 0:4, :], zlb[:, 0:4, 1:513],
                       waits=[zl_m0, pm.get("xb_read")], inc=1)
            xs_m = p.cp(xsb3[:, 4:8, :], zlb[:, 4:8, 0:512], waits=[zl_m],
                        inc=1)
            w = [cpa] + [pm[k] for k in ("lnps_read", "w2b1") if pm.get(k)]
            for dc in range(8):
                p.mm(pk[0][:], ones128, xsb3[:, dc, :], dc == 0, dc == 7,
                     waits=w if dc == 0 else ([xs_m] if dc == 4 else ()))
            sqw = [cpa] + ([pm["lnps_read"]] if pm.get("lnps_read") else [])
            for dc in range(8):
                sm = p.tt(SQT, xsb3[:, dc, :], xsb3[:, dc, :], ALU.mult,
                          waits=(sqw if dc == 0 else
                                 ([xs_m] if dc == 4 else [])) +
                                ([pm["sqt_read"]] if pm.get("sqt_read") else []),
                          inc=1)
                pm["sqt_read"] = p.mm(pk[1][:], ones128, SQT, dc == 0, dc == 7,
                                      waits=[sm], inc=1)
            mst = p.mark("tensor")
            p.ts(MUB, pk[0][:], 1.0 / D, ALU.mult,
                 waits=[mst] + ([pm["mub_read"]] if pm.get("mub_read") else []))
            v0 = p.ts(VARB, pk[1][:], 1.0 / D, ALU.mult, inc=1)
            m2m = p.tt(pk[2][:], MUB, MUB, ALU.mult,
                       waits=[v0] + ([pm["w1b0"]] if pm.get("w1b0") else []),
                       inc=1)
            vv = p.tt(RSB, VARB, pk[2][:], ALU.subtract, waits=[m2m], inc=1)
            vv2 = p.ts(RSB, RSB, 1e-5, ALU.add, waits=[vv], inc=1)
            b1a = p.act(RSB, RSB, AF.Ln, waits=[vv2], inc=1)
            b2a = p.act(RSB, RSB, AF.Exp, scale=-0.5, waits=[b1a], inc=1)
            pm["lnps_read"] = b2a
            xn_m = None
            for dc in range(8):
                p.tt(T1[:], xsb3[:, dc, :], MUB, ALU.subtract,
                     waits=[b2a, pm.get("t1_read")])
                tm = p.mark("vector")
                p.tt(U1[:], T1[:], RSB, ALU.mult, waits=[tm])
                tm2 = p.mark("vector")
                pm["t1_read"] = tm2
                xn_m = p.ts(xn3[:, dc, :], U1[:], VECS[:, 4 + dc:5 + dc],
                            ALU.mult, s2=VECS[:, 12 + dc:13 + dc],
                            op1=ALU.add, waits=[tm2] +
                            ([pm["xn_read"]] if (dc == 0 and pm.get("xn_read"))
                             else []), inc=1)
            pm["mub_read"] = xn_m
            if (li == 0 and ti == 0 and
                    globals().get("_TAP_A") == "xn"):
                bar = p.barrier()
                tdm = None
                for ck in range(8):
                    cc1 = p.cp(Z1T[:, 0:512], XN[:, ck * 512:(ck + 1) * 512],
                               inc=1, waits=[bar[0]] + ([tdm] if tdm else []))
                    tdm = p.dma(globals()["_TAP_AAP"][:, ck * 512:(ck + 1) * 512],
                                Z1T[:, 0:512], waits=[cc1])
                p.barrier()

            gg_m = []
            for ks in range(22):
                wcols = 128 if ks < 21 else 42
                aoff = ks * 128
                goff = DI + ks * 128
                pa = pk[2 + (ks % 2) * 2]
                pg = pk[3 + (ks % 2) * 2]
                w = [xn_m]
                if pm.get(f"w1b{ks % 2}"):
                    w.append(pm[f"w1b{ks % 2}"])
                for dc in range(8):
                    p.mm(pa[0:wcols, :], W1B[:, dc * D1 + aoff:dc * D1 + aoff + wcols],
                         xn3[:, dc, :], dc == 0, dc == 7,
                         waits=w if dc == 0 else ())
                for dc in range(8):
                    p.mm(pg[0:wcols, :], W1B[:, dc * D1 + goff:dc * D1 + goff + wcols],
                         xn3[:, dc, :], dc == 0, dc == 7)
                mw1 = p.mark("tensor")
                ge_m = p.act(GE[0:wcols, :], pg[0:wcols, :], AF.Gelu,
                             bias=VECS[0:wcols, 50 + ks:51 + ks],
                             waits=[mw1] + ([pm["ge_read"]] if pm.get("ge_read")
                                            else []), inc=1)
                ggm = p.stt(GG[0:wcols, ks * 512:(ks + 1) * 512], pa[0:wcols, :],
                            VECS[0:wcols, 28 + ks:29 + ks], GE[0:wcols, :],
                            ALU.add, ALU.mult,
                            waits=[ge_m] + ([pm["gg_read"]] if
                                            (ks == 0 and pm.get("gg_read"))
                                            else []), inc=1)
                pm["ge_read"] = ggm
                pm[f"w1b{ks % 2}"] = ggm
                gg_m.append(ggm)
            if (li == 0 and ti == 0 and
                    globals().get("_TAP_A") == "gg"):
                bar = p.barrier()
                tdm = None
                for ck in range(22):
                    cc1 = p.cp(Z1T[:, 0:512], GG[:, ck * 512:(ck + 1) * 512],
                               inc=1, waits=[bar[0]] + ([tdm] if tdm else []))
                    tdm = p.dma(globals()["_TAP_AAP"][:, ck * 512:(ck + 1) * 512],
                                Z1T[:, 0:512], waits=[cc1])
                p.barrier()
            z2m = []
            qmm = {}
            hld = {}

            def w2qload(q, extra):
                WH = W2BH0 if q % 2 == 0 else W2BH1
                w = wb_st + [e for e in extra if e]
                p.dma(WH.rearrange("p (ks c) -> p ks c", c=256)[0:128, 0:21, :],
                      apc(w2s, q * 256,
                          [(D, 128), (128 * D, 21), (1, 256)]), waits=w)
                p.dma(WH[0:42, :].rearrange("p (ks c) -> p ks c", c=256)
                      [:, 21:22, :],
                      apc(w2s, 21 * 128 * D + q * 256,
                          [(D, 42), (1, 256)]))
                hld[q] = p.mark("sync")
            for q in range(2):
                w2qload(q, [pm.get(f"w2h_read{q % 2}")])
            for q in range(4):
                WH = W2BH0 if q % 2 == 0 else W2BH1
                for och in range(2):
                    oc = q * 2 + och
                    bank = pk[6] if oc % 2 == 0 else pk[0]
                    w = [gg_m[-1], hld[q]]
                    if pm.get(f"w2b{oc % 2}"):
                        w.append(pm[f"w2b{oc % 2}"])
                    for ks in range(22):
                        rows = 128 if ks < 21 else 42
                        p.mm(bank[:], WH[0:rows, ks * 256 + och * 128:
                                         ks * 256 + och * 128 + 128],
                             GG[0:rows, ks * 512:(ks + 1) * 512], ks == 0, ks == 21,
                             waits=w if ks == 0 else ())
                    mw2 = p.mark("tensor")
                    if och == 1:
                        qmm[q] = mw2
                        pm[f"w2h_read{q % 2}"] = mw2
                        if q + 2 <= 3:
                            w2qload(q + 2, [mw2])
                    zw = [mw2]
                    if oc == 0 and z2_dma:
                        zw.append(z2_dma)
                    if oc == 4:
                        zw.append(h0)
                    zz = p.stt(Z1T[:, (oc % 4) * 512:(oc % 4 + 1) * 512], bank[:],
                               VECS[:, 20 + oc:21 + oc],
                               zlb[:, oc, 1:513], ALU.add, ALU.add,
                               waits=zw, inc=1)
                    pm[f"w2b{oc % 2}"] = zz
                    z2m.append(zz)
                    if oc == 3:
                        h0 = p.dma(apc(zdst, 257 + t0,
                                       [(GW, 128), (128 * GW, 4), (1, 512)]),
                                   Z1T[:].rearrange("p (dc c) -> p dc c", c=512),
                                   waits=z2m[0:4])
            tmk = p.mark("tensor")
            pm["xb_read"] = tmk
            pm["gg_read"] = tmk
            pm["w2h_read"] = tmk
            pm["xn_read"] = tmk
            pm["zl_read"] = z2m[-1]
            z2_dma = p.dma(apc(zdst, 257 + t0 + 4 * 128 * GW,
                               [(GW, 128), (128 * GW, 4), (1, 512)]),
                           Z1T[:].rearrange("p (dc c) -> p dc c", c=512),
                           waits=z2m[4:8])
        pm["za_done"] = z2_dma
        pm["wa_read"] = pm["xb_read"]
        pm["v_read"] = z2m[-1]

        if tap == "z" and globals().get("_TAP_Z") == (li, "b"):
            bar = p.barrier()
            p.dma(tap_ap, apc(zb[0], 257, [(GW, D), (1, L)]), waits=[bar[0]])

    p.barrier()
    es.close()



def build_final(nc, p, stack, dr, cst, zb, x1h, out, n_layers):
    es = contextlib.ExitStack()
    sbt = lambda n, s, d: es.enter_context(nc.sbuf_tensor(n, list(s), d))
    pst = lambda n, s, d: es.enter_context(nc.psum_tensor(n, list(s), d))
    identf = cst["identf"]
    ZT = sbt("fZT", (128, 8 * 128), F32)
    X1L = sbt("fX1", (128, D), F32)
    OT = sbt("fOT", (128, D), F32)
    pzf = [pst(f"fpz{i}", (128, 512), F32) for i in range(2)]
    zsrc = zb[0]
    prevv = None
    st_marks = []
    for tc in range(32):
        wl = [prevv] if prevv else ()
        zm = p.dma(ZT[:], apc(zsrc, 257 + tc * 128,
                              [(GW, 128), (128 * GW, 8), (1, 128)]), waits=wl)
        xm = p.dma(X1L[:], x1h[tc * 128:(tc + 1) * 128, :])
        ld = p.mark("sync")
        evs = []
        for grp in range(2):
            bsel = pzf[grp]
            tms = None
            for k in range(4):
                dc = grp * 4 + k
                w = [ld] + (evs[-2:] if evs else [])
                tms = p.tr(bsel[:, k * 128:(k + 1) * 128],
                           ZT[:, dc * 128:(dc + 1) * 128], identf,
                           waits=w, inc=(1 if k == 3 else 0))
            evs.append(p.tt(OT[:, grp * 512:(grp + 1) * 512], bsel[:],
                            X1L[:, grp * 512:(grp + 1) * 512], ALU.add,
                            waits=[tms, xm] + (st_marks[-1:] if st_marks else []),
                            inc=1))
        prevv = evs[-1]
        st_marks.append(p.dma(out[tc * 128:(tc + 1) * 128, :], OT[:],
                              waits=[prevv]))
    p.op("sync", lambda h: h.nop(), waits=[st_marks[-1]])
    es.close()


def kernel(**inputs):
    x = np.ascontiguousarray(inputs["L0_attn_x"])
    consts = host_constants()
    base = {
        "log_dt": np.ascontiguousarray(inputs["L0_attn_log_dt"]),
        "log_A_real": np.ascontiguousarray(inputs["L0_attn_log_A_real"]),
        "A_imag": np.ascontiguousarray(inputs["L0_attn_A_imag"]),
        "C_real": np.ascontiguousarray(inputs["L0_attn_C_real"]),
        "C_imag": np.ascontiguousarray(inputs["L0_attn_C_imag"]),
        "Wssm": np.ascontiguousarray(inputs["L0_attn_W"]),
        "bssm": np.ascontiguousarray(inputs["L0_attn_b"]),
        "Wqkv": np.ascontiguousarray(inputs["L0_ffn_Wqkv"]),
        "q_scale": np.ascontiguousarray(inputs["L0_ffn_q_scale"]),
        "k_scale": np.ascontiguousarray(inputs["L0_ffn_k_scale"]),
        "Wo": np.ascontiguousarray(inputs["L0_ffn_Wo"]),
        "ln_g": np.ascontiguousarray(inputs["L0_ffn_ln_g"]),
        "ln_b": np.ascontiguousarray(inputs["L0_ffn_ln_b"]),
        "W1": np.ascontiguousarray(inputs["L0_ffn_W1"]),
        "b1": np.ascontiguousarray(inputs["L0_ffn_b1"]),
        "W2": np.ascontiguousarray(inputs["L0_ffn_W2"]),
        "b2": np.ascontiguousarray(inputs["L0_ffn_b2"]),
    }
    base.update(consts)
    nc = build_program()
    in_maps = []
    for b in range(NCORES):
        m = dict(base)
        m["x"] = np.ascontiguousarray(x[b])
        in_maps.append(m)
    res = run_bass_kernel_spmd(nc, in_maps, core_ids=list(range(NCORES)))
    return np.stack([np.asarray(r["out"]) for r in res.results], axis=0)


# version 18
# speedup vs baseline: 1.0180x; 1.0180x over previous
import contextlib
import math
import numpy as np
import ml_dtypes
import concourse.bass as bass
import concourse.mybir as mybir
import bass_rust
from concourse.bass_utils import run_bass_kernel_spmd

F32 = mybir.dt.float32
BF16 = mybir.dt.bfloat16
AF = mybir.ActivationFunctionType
ALU = mybir.AluOpType
AX = mybir.AxisListType
BFD = ml_dtypes.bfloat16

NCORES = 8
L = 4096
D = 1024
NST = 32
CC = 64
NCH = L // CC
HEADS = 16
DH = 64
WIN = 256
DI = 2730
DQKV = 3072
D1 = 5460
TW = 512
NTILE = L // TW
GW = 257 + L + 27
DEPTH = 6
NEGBIG = -1.0e38
LN8 = math.log(8.0)

ENGS = ("sync", "tensor", "vector", "scalar", "gpsimd")
CONV_GROUPS = 4
CONV_PH = 15


def vp(*dims):
    return bass_rust.VecI64Pair([list(d) for d in dims])


def apc(base_ap, offset_elems, dims):
    a = base_ap.copy()
    a.offset = offset_elems
    a.ap = vp(*dims)
    return a


class Prog:
    def __init__(self, nc, stack):
        self.nc = nc
        self.stream = {e: [] for e in ENGS}
        self.sem = {e: stack.enter_context(nc.semaphore(f"s_{e}")) for e in ENGS}
        self.cnt = {e: 0 for e in ENGS}
        self.last = {}
        self.water = {e: {o: 0 for o in ENGS} for e in ENGS}

    def op(self, eng, fn, inc=0, waits=(), dma=False):
        wl = []
        for w in waits:
            if w is None:
                continue
            we, wv = w
            if wv > self.water[eng][we]:
                self.water[eng][we] = wv
                wl.append((we, wv))
        rec = {"fn": fn, "wl": tuple(wl), "inc": inc, "dma": dma}
        self.stream[eng].append(rec)
        self.last[eng] = rec
        if inc:
            self.cnt[eng] += inc
        return (eng, self.cnt[eng])

    def dma(self, out, in_, waits=(), eng="sync", **kw):
        return self.op(eng, lambda h: h.dma_start(out=out, in_=in_, **kw), inc=16,
                       waits=waits, dma=True)

    def mark(self, eng):
        rec = self.last.get(eng)
        if rec is not None and not rec["inc"]:
            rec["inc"] = 16 if rec["dma"] else 1
            self.cnt[eng] += rec["inc"]
        return (eng, self.cnt[eng])

    def barrier(self, excl=("gpsimd",)):
        marks = [self.mark(e) for e in ENGS if e not in excl]
        out = []
        for e in ENGS:
            if e in excl:
                continue
            out.append(self.op(e, lambda h: h.nop(), inc=1, waits=marks))
        return out

    def tt(self, out, a, b, op, waits=(), inc=0, eng="vector"):
        return self.op(eng, lambda v: v.tensor_tensor(out=out, in0=a, in1=b, op=op),
                       inc=inc, waits=waits)

    def ts(self, out, a, s1, op0, s2=None, op1=None, waits=(), inc=0, eng="vector",
           accum_out=None):
        kw = {}
        if op1 is not None:
            kw["op1"] = op1
        if accum_out is not None:
            kw["accum_out"] = accum_out
        return self.op(eng, lambda v: v.tensor_scalar(out=out, in0=a, scalar1=s1,
                       scalar2=s2, op0=op0, **kw), inc=inc, waits=waits)

    def stt(self, out, a, s, b, op0, op1, waits=(), inc=0):
        return self.op("vector", lambda v: v.scalar_tensor_tensor(out=out, in0=a,
                       scalar=s, in1=b, op0=op0, op1=op1), inc=inc, waits=waits)

    def cp(self, out, a, waits=(), inc=0, eng="vector"):
        if eng == "scalar":
            return self.op(eng, lambda h: h.copy(out=out, in_=a), inc=inc,
                           waits=waits)
        return self.op(eng, lambda v: v.tensor_copy(out=out, in_=a), inc=inc,
                       waits=waits)

    def mset(self, ap, val, waits=(), inc=0, eng="vector"):
        return self.op(eng, lambda v: v.memset(ap, val), inc=inc, waits=waits)

    def red(self, out, a, op=ALU.max, negate=False, waits=(), inc=1):
        def fn(v):
            return v.tensor_reduce(out=out, in_=a, axis=AX.X, op=op, negate=negate)
        return self.op("vector", fn, inc=inc, waits=waits)

    def act(self, out, a, func, bias=0.0, scale=1.0, accum_out=None, waits=(), inc=0):
        kw = {}
        if accum_out is not None:
            kw["accum_out"] = accum_out
        return self.op("scalar", lambda s: s.activation(out=out, in_=a, func=func,
                       bias=bias, scale=scale, **kw), inc=inc, waits=waits)

    def mm(self, out, lhsT, rhs, start, stop, waits=(), inc=0, tile_position=None):
        kw = {}
        if tile_position is not None:
            kw["tile_position"] = tile_position
        return self.op("tensor", lambda t: t.matmul(out=out, lhsT=lhsT, rhs=rhs,
                       start=start, stop=stop, **kw), inc=inc, waits=waits)

    def tr(self, out, in_, ident, waits=(), inc=0):
        return self.op("tensor", lambda t: t.transpose(out=out, in_=in_,
                       identity=ident), inc=inc, waits=waits)



def host_constants():
    c = {}
    c["identb"] = np.eye(128, dtype=BFD)
    c["identf"] = np.eye(128, dtype=np.float32)
    perm = np.zeros((128, 128), dtype=np.float32)
    for b in (0, 64):
        for m in range(32):
            perm[b + m + 32, b + m] = -1.0
            perm[b + m, b + m + 32] = 1.0
    c["permq"] = perm.astype(BFD)
    i = np.arange(128)[:, None]
    j = np.arange(128)[None, :]
    mask = np.zeros((128, 256), dtype=np.float32)
    mask[:, 0:128] = np.where(j < i, NEGBIG, 0.0)
    mask[:, 128:256] = np.where(j > i, NEGBIG, 0.0)
    c["maskt"] = mask.astype(BFD)
    m3 = np.zeros((128, 384), dtype=np.float32)
    m3[:, 0:128] = mask[:, 0:128]
    m3[:, 256:384] = mask[:, 128:256]
    c["maskt3"] = m3.astype(BFD)
    half = np.arange(0, DH, 2, dtype=np.float32)
    inv_freq = 1.0 / 10000.0 ** (half / DH)
    t = np.arange(L, dtype=np.float32)
    freqs = t[:, None] * inv_freq[None, :]
    freqs = np.concatenate([freqs, freqs], -1)
    base_scale = (half + 0.4 * DH) / (1.4 * DH)
    power = (t - L // 2) / (WIN // 2)
    scale = base_scale[None, :] ** power[:, None]
    scale = np.concatenate([scale, scale], -1)
    cos, sin = np.cos(freqs), np.sin(freqs)
    rep = lambda a: np.concatenate([a.T, a.T], axis=0).astype(BFD)
    c["csq"] = rep(cos * scale)
    c["snq"] = rep(sin * scale)
    c["csk"] = rep(cos / scale)
    c["snk"] = rep(sin / scale)
    c["ones128"] = np.ones((128, 128), dtype=BFD)
    for ch in range(8):
        m = np.zeros((128, 16), dtype=np.float32)
        m[0:64, 2 * ch] = 1.0
        m[64:128, 2 * ch + 1] = 1.0
        c[f"bd16_{ch}"] = m.astype(BFD)
        q = np.zeros((16, 128), dtype=np.float32)
        q[2 * ch, 0:64] = 1.0
        q[2 * ch + 1, 64:128] = 1.0
        c[f"bcp_{ch}"] = q.astype(BFD)
    return c



def build_tables(nc, p, stack, dr, gbufh, vbufh, ebufh, Lr_t, Li_t, cst):
    es = contextlib.ExitStack()
    sbt = lambda n, s, d: es.enter_context(nc.sbuf_tensor(n, list(s), d))
    pst = lambda n, s, d: es.enter_context(nc.psum_tensor(n, list(s), d))

    prm = sbt("tb_prm", (128, 4360), F32)
    ldt_v = prm[:, 0:8]
    dtrep = prm[:, 8:264].rearrange("p (c n) -> p c n", n=NST)
    arr = prm[:, 264:520].rearrange("p (c n) -> p c n", n=NST)
    aii = prm[:, 520:776].rearrange("p (c n) -> p c n", n=NST)
    dtar = prm[:, 776:1032].rearrange("p (c n) -> p c n", n=NST)
    dtai = prm[:, 1032:1288].rearrange("p (c n) -> p c n", n=NST)
    lamr = prm[:, 1288:1544].rearrange("p (c n) -> p c n", n=NST)
    lami = prm[:, 1544:1800].rearrange("p (c n) -> p c n", n=NST)
    l64r = prm[:, 1800:2056].rearrange("p (c n) -> p c n", n=NST)
    l64i = prm[:, 2056:2312].rearrange("p (c n) -> p c n", n=NST)
    crp = prm[:, 2312:2568].rearrange("p (c n) -> p c n", n=NST)
    cip = prm[:, 2568:2568 + 256].rearrange("p (c n) -> p c n", n=NST)
    t1 = prm[:, 2824:2824 + 256].rearrange("p (c n) -> p c n", n=NST)
    t2 = prm[:, 3080:3080 + 256].rearrange("p (c n) -> p c n", n=NST)
    t3 = prm[:, 3336:3336 + 256].rearrange("p (c n) -> p c n", n=NST)
    halfpi = sbt("tb_hp", (128, 1), F32)
    hp_m = p.mset(halfpi[:], math.pi / 2, inc=1)

    ld = [p.dma(ldt_v, apc(dr["log_dt"], 0, [(1, 128), (128, 8)]),
                allow_slow_non_contiguous=True)]
    for name, dst in (("log_A_real", arr), ("A_imag", aii),
                      ("C_real", crp), ("C_imag", cip)):
        ld.append(p.dma(dst, apc(dr[name], 0, [(NST, 128), (128 * NST, 8), (1, NST)])))
    lds = p.mark("sync")

    p.act(ldt_v, ldt_v, AF.Exp, waits=[lds])
    sc0 = p.mark("scalar")
    for j in range(NST):
        p.cp(dtrep[:, :, j], ldt_v, waits=[sc0])
    p.act(prm[:, 264:520], prm[:, 264:520], AF.Exp, waits=[lds])
    sc1 = p.mark("scalar")
    p.ts(prm[:, 264:520], prm[:, 264:520], -1.0, ALU.mult, waits=[sc1])
    p.tt(dtar, arr, dtrep, ALU.mult)
    p.tt(dtai, aii, dtrep, ALU.mult)
    src = dtai
    for _ in range(2):
        p.ts(t2, src, math.pi, ALU.is_gt)
        p.stt(t1, t2, -2 * math.pi, src, ALU.mult, ALU.add)
        src = t1
    vm = p.mark("vector")
    ab_m = p.act(prm[:, 4104:4360], prm[:, 2824:3080], AF.Abs, waits=[vm, hp_m],
                 inc=1)
    p.ts(prm[:, 3592:3848], prm[:, 4104:4360], -1.0, ALU.mult,
         s2=math.pi / 2, op1=ALU.add, waits=[ab_m])
    cv_m = p.mark("vector")
    p.act(prm[:, 3592:3848], prm[:, 3592:3848], AF.Sin, waits=[cv_m])
    p.act(prm[:, 3848:4104], prm[:, 2824:3080], AF.Sin, waits=[vm])
    p.act(prm[:, 1288:1544], prm[:, 776:1032], AF.Exp, waits=[vm])
    sc2 = p.mark("scalar")
    p.tt(lami, lamr, prm[:, 3848:4104].rearrange("p (c n) -> p c n", n=NST),
         ALU.mult, waits=[sc2])
    p.tt(lamr, lamr, prm[:, 3592:3848].rearrange("p (c n) -> p c n", n=NST),
         ALU.mult)
    p.tt(t1, arr, arr, ALU.mult)
    p.stt(t2, aii, 1.0, aii, ALU.mult, ALU.mult)
    p.tt(t1, t1, t2, ALU.add)
    p.op("vector", lambda v: v.reciprocal(out=qtmp[:, 512:768], in_=prm[:, 2824:3080]))
    p.ts(t2, lamr, -1.0, ALU.add)
    p.tt(t3, t2, arr, ALU.mult)
    qtmp = sbt("tb_q", (128, 3 * 256), F32)
    qr = qtmp[:, 0:256].rearrange("p (c n) -> p c n", n=NST)
    qi = qtmp[:, 256:512].rearrange("p (c n) -> p c n", n=NST)
    vinv = qtmp[:, 512:768].rearrange("p (c n) -> p c n", n=NST)
    p.tt(qr, lami, aii, ALU.mult)
    p.tt(qr, qr, t3, ALU.add)
    p.tt(qr, qr, vinv, ALU.mult)
    p.tt(t3, lami, arr, ALU.mult)
    p.tt(qi, t2, aii, ALU.mult)
    p.tt(qi, t3, qi, ALU.subtract)
    p.tt(qi, qi, vinv, ALU.mult)
    p.tt(t1, crp, qr, ALU.mult)
    p.tt(t2, cip, qi, ALU.mult)
    p.tt(t3, crp, qi, ALU.mult)
    p.tt(qi, cip, qr, ALU.mult)
    p.tt(crp, t1, t2, ALU.subtract)
    p.tt(cip, t3, qi, ALU.add)
    p.ts(prm[:, 2312:2568], prm[:, 2312:2568], 2.0, ALU.mult)
    p.ts(prm[:, 2568:2824], prm[:, 2568:2824], 2.0, ALU.mult)
    p.cp(prm[:, 3080:3336], qtmp[:, 0:256])
    p.cp(prm[:, 3336:3592], qtmp[:, 512:768])
    cdone = p.mark("vector")

    POW = sbt("tb_POW", (128, 2 * 2 * 65 * NST), F32)
    pr3 = POW[:, 0:4160].rearrange("p (c u n) -> p c u n", c=2, n=NST)
    pi3 = POW[:, 4160:8320].rearrange("p (c u n) -> p c u n", c=2, n=NST)
    REP = sbt("tb_REP", (128, 2 * 2 * 64 * NST), F32)
    rr3 = REP[:, 0:4096].rearrange("p (c u n) -> p c u n", c=2, n=NST)
    ri3 = REP[:, 4096:8192].rearrange("p (c u n) -> p c u n", c=2, n=NST)
    SCR = sbt("tb_SCR", (128, 2 * 2 * 64 * NST), F32)
    sa3 = SCR[:, 0:4096].rearrange("p (c u n) -> p c u n", c=2, n=NST)
    sb3 = SCR[:, 4096:8192].rearrange("p (c u n) -> p c u n", c=2, n=NST)
    CRE = sbt("tb_CRE", (128, 2 * 2 * 64 * NST), F32)
    cr3 = CRE[:, 0:4096].rearrange("p (c u n) -> p c u n", c=2, n=NST)
    ci3 = CRE[:, 4096:8192].rearrange("p (c u n) -> p c u n", c=2, n=NST)
    CAST = sbt("tb_CAST", (128, 4160), BF16)
    KROW = sbt("tb_KROW", (128, 128), F32)

    prev_dma = None
    for pr in range(4):
        dc0 = 2 * pr
        ws = [cdone] + ([prev_dma] if prev_dma else [])
        for k in (0, 1):
            p.mset(pr3[:, k, 0, :], 1.0, waits=ws)
            p.mset(pi3[:, k, 0, :], 0.0)
            p.cp(rr3[:, k, 0, :], lamr[:, dc0 + k, :])
            p.cp(ri3[:, k, 0, :], lami[:, dc0 + k, :])
        for j in range(6):
            n = 1 << j
            ps_r, ps_i = pr3[:, :, 0:n, :], pi3[:, :, 0:n, :]
            nn_r, nn_i = pr3[:, :, n:2 * n, :], pi3[:, :, n:2 * n, :]
            R_r, R_i = rr3[:, :, 0:n, :], ri3[:, :, 0:n, :]
            A_, B_ = sa3[:, :, 0:n, :], sb3[:, :, 0:n, :]
            p.tt(A_, ps_r, R_r, ALU.mult)
            p.tt(B_, ps_i, R_i, ALU.mult)
            p.tt(nn_r, A_, B_, ALU.subtract)
            p.tt(A_, ps_r, R_i, ALU.mult)
            p.tt(B_, ps_i, R_r, ALU.mult)
            p.tt(nn_i, A_, B_, ALU.add)
            p.tt(A_, R_r, R_r, ALU.mult)
            p.tt(B_, R_i, R_i, ALU.mult)
            p.stt(R_i, R_r, 2.0, R_i, ALU.mult, ALU.mult)
            p.tt(R_r, A_, B_, ALU.subtract)
            if j < 5:
                p.cp(rr3[:, :, n:2 * n, :], R_r)
                p.cp(ri3[:, :, n:2 * n, :], R_i)
        p.cp(pr3[:, :, 64, :], rr3[:, :, 0, :])
        p.cp(pi3[:, :, 64, :], ri3[:, :, 0, :])
        for k in (0, 1):
            p.cp(l64r[:, dc0 + k, :], rr3[:, k, 0, :])
            p.cp(l64i[:, dc0 + k, :], ri3[:, k, 0, :])
        for k in (0, 1):
            p.cp(cr3[:, k, 0, :], crp[:, dc0 + k, :])
            p.cp(ci3[:, k, 0, :], cip[:, dc0 + k, :])
        for j in range(6):
            n = 1 << j
            p.cp(cr3[:, :, n:2 * n, :], cr3[:, :, 0:n, :])
            p.cp(ci3[:, :, n:2 * n, :], ci3[:, :, 0:n, :])
        p.tt(rr3, cr3, pr3[:, :, 0:64, :], ALU.mult)
        p.tt(ri3, ci3, pi3[:, :, 0:64, :], ALU.mult)
        p.tt(rr3, rr3, ri3, ALU.subtract)
        rm = p.red(KROW[:].rearrange("p (c r) -> p c r", c=2), rr3, op=ALU.add)
        p.cp(CAST[:, 0:128], KROW[:], waits=[rm])
        vk = p.mark("vector")
        kd = p.dma(apc(gbufh, (dc0 * 128) * 128 + 63,
                       [(128, 128), (128 * 128, 2), (1, 64)]),
                   CAST[:, 0:128].rearrange("p (c r) -> p c r", c=2), waits=[vk])
        p.cp(CAST[:, 0:4096], POW[:, 0:4160].rearrange(
            "p (c u n) -> p c u n", c=2, n=NST)[:, :, 0:64, :], waits=[kd])
        vc1 = p.mark("vector")
        vd1 = None
        for k in range(2):
            vd1 = p.dma(apc(vbufh, ((dc0 + k) * 128) * (CC * 64),
                            [(CC * 64, 128), (64, CC), (1, NST)]),
                        CAST[:, k * 2048:(k + 1) * 2048].rearrange(
                            "p (u n) -> p u n", n=NST), waits=[vc1])
        p.cp(CAST[:, 0:4096], POW[:, 4160:8320].rearrange(
            "p (c u n) -> p c u n", c=2, n=NST)[:, :, 0:64, :], waits=[vd1])
        vc2 = p.mark("vector")
        vd2 = None
        for k in range(2):
            vd2 = p.dma(apc(vbufh, ((dc0 + k) * 128) * (CC * 64) + NST,
                            [(CC * 64, 128), (64, CC), (1, NST)]),
                        CAST[:, k * 2048:(k + 1) * 2048].rearrange(
                            "p (u n) -> p u n", n=NST), waits=[vc2])
        shf_r = pr3[:, :, 1:65, :]
        shf_i = pi3[:, :, 1:65, :]
        p.tt(rr3, cr3, shf_r, ALU.mult, waits=[vd2])
        p.tt(ri3, ci3, shf_i, ALU.mult)
        p.tt(rr3, rr3, ri3, ALU.subtract)
        p.cp(CAST[:, 0:4096].rearrange("p (c n r) -> p c r n", c=2, n=NST),
             REP[:, 0:4096].rearrange("p (c r n) -> p c r n", c=2, n=NST))
        ve1 = p.mark("vector")
        ed1 = None
        for k in range(2):
            ed1 = p.dma(apc(ebufh, ((dc0 + k) * 128) * (2 * NST * CC),
                            [(2 * NST * CC, 128), (CC, NST), (1, CC)]),
                        CAST[:, k * 2048:(k + 1) * 2048].rearrange(
                            "p (n r) -> p n r", n=NST), waits=[ve1])
        p.tt(rr3, cr3, shf_i, ALU.mult, waits=[ed1])
        p.tt(ri3, ci3, shf_r, ALU.mult)
        p.tt(rr3, rr3, ri3, ALU.add)
        p.ts(REP[:, 0:4096], REP[:, 0:4096], -1.0, ALU.mult)
        p.cp(CAST[:, 0:4096].rearrange("p (c n r) -> p c r n", c=2, n=NST),
             REP[:, 0:4096].rearrange("p (c r n) -> p c r n", c=2, n=NST))
        ve2 = p.mark("vector")
        ed2 = None
        for k in range(2):
            ed2 = p.dma(apc(ebufh, ((dc0 + k) * 128) * (2 * NST * CC) + NST * CC,
                            [(2 * NST * CC, 128), (CC, NST), (1, CC)]),
                        CAST[:, k * 2048:(k + 1) * 2048].rearrange(
                            "p (n r) -> p n r", n=NST), waits=[ve2])
        prev_dma = ed2

    ZT = sbt("tb_ZT", (128, 64), BF16)
    zm = p.mset(ZT[:], 0.0, inc=1)
    for dc in range(8):
        p.dma(apc(gbufh, (dc * 128) * 128, [(128, 128), (1, 63)]), ZT[:, 0:63],
              waits=[zm])
        p.dma(apc(gbufh, (dc * 128) * 128 + 127, [(128, 128), (1, 1)]), ZT[:, 0:1],
              waits=[zm], allow_slow_non_contiguous=True)

    pL = pst("tb_pL", (32, 128), F32)
    idf = cst["identf"]
    vnow = p.mark("vector")
    lastcp = vnow
    for dc in range(8):
        g, sub = dc // 2, dc % 2
        for comp, dstT in ((0, Lr_t), (1, Li_t)):
            base = 1800 if comp == 0 else 2056
            tm = p.tr(pL[:], prm[:, base + dc * NST: base + (dc + 1) * NST],
                      idf, waits=[vnow, lastcp], inc=1)
            for rho in (0, 1):
                lastcp = p.cp(dstT[g][rho * 64 + sub * 32: rho * 64 + sub * 32 + 32,
                              :], pL[:], waits=[tm], inc=1)
    if globals().get("_TAP_PRM") is not None:
        p.dma(globals()["_TAP_PRM"], prm[:])
    done = p.barrier()
    es.close()
    return done



def build_program(n_layers=DEPTH, tap=None, stages=(1, 2, 3)):
    stack = contextlib.ExitStack()
    nc = bass.Bass("TRN2", target_bir_lowering=False)

    dr = {}
    def din(name, shape, dt=F32):
        dr[name] = nc.dram_tensor(name, list(shape), dt, kind="ExternalInput").ap()
        return dr[name]

    din("x", (L, D))
    din("log_dt", (D,))
    din("log_A_real", (D, NST))
    din("A_imag", (D, NST))
    din("C_real", (D, NST))
    din("C_imag", (D, NST))
    din("Wssm", (D, D))
    din("bssm", (D,))
    din("Wqkv", (DEPTH, D, DQKV))
    din("q_scale", (DEPTH, DH))
    din("k_scale", (DEPTH, DH))
    din("Wo", (DEPTH, D, D))
    din("ln_g", (DEPTH, D))
    din("ln_b", (DEPTH, D))
    din("W1", (DEPTH, D, D1))
    din("b1", (DEPTH, D1))
    din("W2", (DEPTH, DI, D))
    din("b2", (DEPTH, D))
    consts = host_constants()
    for k, v in consts.items():
        din(k, v.shape, BF16 if v.dtype == BFD else F32)

    out = nc.dram_tensor("out", [L, D], F32, kind="ExternalOutput").ap()
    tap_ap = None
    g = globals()
    g.setdefault("_TAP_PRM", None)
    if tap is not None:
        tshape = {"x1": (L, D), "yb": (D, L), "z0": (D, L), "z": (D, L),
                  "gbuf": (D, 128), "vbuf": (D, 128), "ebuf": (D, 128),
                  "qf": (128, 4096), "kw": (128, 4096), "ao": (128, 4096),
                  "vw": (128, 4096), "pb": (128, 6144), "ao0": (128, 4096), "xn": (128, 4096), "gg": (128, 22 * 512),
                  "prm": (128, 4360),
                  "ub": (128, 32 * D),
                  "sstb": (128, CC * 128), "pstt": (128, CC * 128)}[tap]
        tap_ap = nc.dram_tensor("tap", list(tshape), F32, kind="ExternalOutput").ap()

    zb = [nc.dram_tensor(f"zbuf{i}", [D, GW], F32, kind="Internal").ap()
          for i in range(2)]
    x1h = nc.dram_tensor("x1h", [L, D], F32, kind="Internal").ap()
    ybh = nc.dram_tensor("ybh", [D, L], BF16, kind="Internal").ap()
    gbufh = nc.dram_tensor("gbufh", [D, 128], BF16, kind="Internal").ap()
    vbufh = nc.dram_tensor("vbufh", [D, CC, 2 * NST], BF16, kind="Internal").ap()
    ebufh = nc.dram_tensor("ebufh", [D, 2 * NST, CC], BF16, kind="Internal").ap()
    w2s = nc.dram_tensor("w2s", [DI, D], BF16, kind="Internal").ap()

    p = Prog(nc, stack)
    g.setdefault("_TAP_PSTT", None)
    g.setdefault("_TAP_UB", None)
    g.setdefault("_TAP_SSTB", None)
    g.setdefault("_TAP_Z", None)
    g.setdefault("_TAP_A", None)
    g.setdefault("_TAP_AAP", None)
    if tap == "prm":
        g["_TAP_PRM"] = tap_ap
    if tap == "pstt":
        g["_TAP_PSTT"] = tap_ap
    if tap == "ub":
        g["_TAP_UB"] = tap_ap
    if tap == "sstb":
        g["_TAP_SSTB"] = tap_ap
    if tap in ("qf", "kw", "ao", "vw", "pb", "ao0", "xn", "gg"):
        g["_TAP_A"] = tap
        g["_TAP_AAP"] = tap_ap

    cst = {}
    for k, v in consts.items():
        if k in ("csq", "snq", "csk", "snk") or k.startswith(("bd16_", "bcp_")):
            continue
        dt = BF16 if v.dtype == BFD else F32
        cst[k] = stack.enter_context(nc.sbuf_tensor("c_" + k, list(v.shape), dt))[:]
        p.dma(cst[k], dr[k])

    es_s0 = contextlib.ExitStack()
    Lr_t = [es_s0.enter_context(nc.sbuf_tensor(f"Lr{g}", [128, 128], F32))[:]
            for g in range(4)]
    Li_t = [es_s0.enter_context(nc.sbuf_tensor(f"Li{g}", [128, 128], F32))[:]
            for g in range(4)]

    if 1 in stages:
        build_tables(nc, p, stack, dr, gbufh, vbufh, ebufh, Lr_t, Li_t, cst)
    if 2 in stages:
        build_conv(nc, p, stack, dr, cst, gbufh, vbufh, ebufh, Lr_t, Li_t, ybh)
    es_s0.close()
    if 3 in stages:
        build_proj(nc, p, stack, dr, cst, ybh, x1h, zb, tap, tap_ap)

    if n_layers > 0:
        build_layers(nc, p, stack, dr, cst, zb, n_layers, tap, tap_ap, w2s)

    if tap in ("gbuf",):
        with nc.sbuf_tensor("dbg_b", [128, 1024], BF16) as DB, \
             nc.sbuf_tensor("dbg_f", [128, 1024], F32) as DF:
            bar = p.barrier()
            pm = None
            for blk in range(8):
                d1 = p.dma(DB[:, 0:128], apc(gbufh, blk * 128 * 128,
                           [(128, 128), (1, 128)]),
                           waits=[bar[0]] if pm is None else [pm])
                cm = p.cp(DF[:, 0:128], DB[:, 0:128], waits=[d1], inc=1)
                pm = p.dma(apc(tap_ap, blk * 128 * 128,
                               [(128, 128), (1, 128)]), DF[:, 0:128], waits=[cm])
    build_final(nc, p, stack, dr, cst, zb, x1h, out, n_layers)

    blk = stack.enter_context(nc.Block())
    for eng in ENGS:
        sec = getattr(blk, eng)

        def make(stream, eng):
            sems = p.sem

            def _(h):
                for rec in stream:
                    for (we, wv) in rec["wl"]:
                        h.wait_ge(sems[we], wv)
                    call = rec["fn"](h)
                    if rec["inc"]:
                        call.then_inc(sems[eng], rec["inc"])
            return _
        sec(make(p.stream[eng], eng))
    stack.close()
    return nc




def build_conv(nc, p, stack, dr, cst, gbufh, vbufh, ebufh, Lr_t, Li_t, ybh):
    es = contextlib.ExitStack()
    sbt = lambda n, s, d: es.enter_context(nc.sbuf_tensor(n, list(s), d))
    pst = lambda n, s, d: es.enter_context(nc.psum_tensor(n, list(s), d))

    UB = sbt("UB", (128, 32 * D), BF16)
    es_ln = contextlib.ExitStack()
    XL = es_ln.enter_context(nc.sbuf_tensor("XL", [128, 2 * D], F32))
    STAT = es_ln.enter_context(nc.sbuf_tensor("STAT", [128, 24], F32))
    SCR0 = es_ln.enter_context(nc.sbuf_tensor("SCR0", [128, D], BF16))

    xin = dr["x"]
    prev = None
    for tt in range(32):
        buf = XL[:, (tt % 2) * D:(tt % 2) * D + D]
        src = apc(xin, (tt * 128 + 63) * D, [(64 * D, 2), (-D, 64), (1, D)])
        dm = p.dma(buf, src, waits=[prev] if prev else ())
        sx, sxx = STAT[:, 0:1], STAT[:, 1:2]
        mu, rs, ep = STAT[:, 2:3], STAT[:, 3:4], STAT[:, 4:5]
        m1 = p.ts(SCR0[:, 0:D], buf, 1.0, ALU.mult, s2=None, op1=ALU.add,
                  accum_out=sx, waits=[dm], inc=1)
        m2 = p.act(SCR0[:, 0:D], buf, AF.Square, accum_out=sxx, waits=[dm], inc=1)
        var_ = STAT[:, 5:6]
        lns = STAT[:, 8:16]
        lns2 = STAT[:, 16:24]
        p.ts(mu, sx, 1.0 / D, ALU.mult, waits=[m1, m2])
        w0 = p.ts(ep, sxx, 1.0 / D, ALU.mult, inc=1)
        p.stt(var_, mu, mu, ep, ALU.mult, ALU.subtract, waits=[w0], inc=1)
        w1 = p.mark("vector")
        for _k in range(8):
            p.ts(STAT[:, 8 + _k:9 + _k], var_, -1.0, ALU.mult, s2=1e-5,
                 op1=ALU.add, waits=[w1])
        vr = p.mark("vector")
        a1 = p.act(lns, lns, AF.Ln, waits=[vr], inc=1)
        a2 = p.act(lns2, lns, AF.Exp, scale=-0.5, waits=[a1], inc=1)
        rs = STAT[:, 16:17]
        prev = p.ts(UB[:, tt * D:(tt + 1) * D], buf, mu, ALU.subtract, s2=rs,
                    op1=ALU.mult, waits=[a2], inc=1)
    if globals().get("_TAP_UB") is not None:
        tdm = None
        for ck in range(16):
            c1 = p.cp(XL[:, 0:2048], UB[:, ck * 2048:(ck + 1) * 2048], inc=1,
                      waits=[tdm] if tdm else ())
            tdm = p.dma(globals()["_TAP_UB"][:, ck * 2048:(ck + 1) * 2048], XL[:, 0:2048],
                        waits=[c1])
    u_done = p.barrier()[0]
    es_ln.close()

    HS = [sbt(f"HS{i}", (128, 64 * 128), BF16) for i in range(2)]
    ES2 = [sbt(f"ES2{i}", (128, 64 * 64), BF16) for i in range(2)]
    PSTT = sbt("PSTT", (128, CC * 128), F32)
    SSTB = sbt("SSTB", (128, CC * 128), BF16)
    SRUN = sbt("SRUN", (128, 3 * 128), F32)
    YSTA = sbt("YSTA", (128, 256 * CC), BF16)
    YST2 = YSTA
    YBS = sbt("YBS", (128, 16 * CC), BF16)
    pbank = [pst(f"cvps{i}", (128, 512), F32) for i in range(8)]
    ptr = pbank[0][:].bitcast(BF16)

    identb = cst["identb"]
    ub_e = UB[0:64, :].rearrange("p (tt d) -> p tt d", d=D)
    ub_o = UB[64:128, :].rearrange("p (tt d) -> p tt d", d=D)
    ysta3 = YSTA[0:64, :].rearrange("p (ch s) -> p ch s", s=CC)
    yst23 = YST2[0:64, :].rearrange("p (ch s) -> p ch s", s=CC)

    p.mset(YSTA[64:128, :], 0.0, waits=[u_done])
    bank_free = [None] * 8
    hs_free = [None, None]
    es_free = [None, None]
    ph3_prev = None
    ph3_dma = None

    for g in range(CONV_GROUPS):
        pevac = []
        yevac = []
        for wave in range(4 if (CONV_PH & 1) else 0):
            ch0 = wave * 64
            dglob0 = g * 256 + ch0
            sub = ch0 // 128
            hs = HS[wave % 2]
            wlist = [u_done]
            if hs_free[wave % 2] is not None:
                wlist.append(hs_free[wave % 2])
            hs3 = hs[:, :].rearrange("p (c x) -> p c x", x=128)
            for half in range(2):
                p.dma(hs3[64 * half:64 * half + 64, :, 0:64],
                      apc(gbufh, dglob0 * 128, [(1, 64), (128, 64), (1, 64)]),
                      waits=wlist)
                p.dma(hs3[64 * half:64 * half + 64, :, 64:128],
                      apc(vbufh, dglob0 * CC * 64,
                          [(64, 64), (CC * 64, 64), (1, 64)]))
            ld = p.mark("sync")
            mm_marks = [None] * 8
            for c in range(64):
                ch = ch0 + c
                dglob = g * 256 + ch
                b = c % 4
                blk = c // 4
                w = [ld]
                for bb in (b, b + 4):
                    if bank_free[bb] is not None:
                        w.append(bank_free[bb])
                p.mm(pbank[b][:, blk * 32:blk * 32 + 32],
                     hs3[0:64, c, :], ub_e[:, :, dglob], True, True, waits=w,
                     tile_position=(0, 0))
                mm_marks[c % 4] = p.mm(
                    pbank[4 + b][:, blk * 32:blk * 32 + 32],
                    hs3[64:128, c, :], ub_o[:, :, dglob], True, True, inc=1,
                    tile_position=(64, 0))
            pst3 = PSTT[:, :].rearrange("p (s q) -> p s q", q=128)
            for b in range(4):
                q0 = (ch0 % 128) + b
                for par in range(2):
                    bank = pbank[b + 4 * par]
                    ye = p.cp(ysta3[:, ch0 + b:ch0 + b + 61:4,
                                    par * 32:par * 32 + 32],
                              bank[0:64, 0:512].rearrange(
                                  "p (k s) -> p k s", s=32),
                              waits=[mm_marks[b]], inc=1, eng="scalar")
                    yevac.append(ye)
                    for rho in range(2):
                        src = bank[64 + rho * 32:96 + rho * 32, 0:512].rearrange(
                            "p (k c2) -> p k c2", c2=32)
                        dst = pst3[rho * 64 + sub * 32: rho * 64 + sub * 32 + 32,
                                   par::2, q0:q0 + 61:4].transpose([0, 2, 1])
                        pe = p.cp(dst, src, waits=[mm_marks[b]], inc=1)
                        pevac.append(pe)
                    bank_free[b + 4 * par] = pevac[-1]
            hs_free[wave % 2] = p.mark("tensor")
        ph1v = p.mark("vector")
        ph1s = p.mark("scalar")
        if g == 0 and globals().get("_TAP_PSTT") is not None:
            p.dma(globals()["_TAP_PSTT"], PSTT[:], waits=[ph1v])

        S = SRUN[:, 0:128]
        T = SRUN[:, 128:256]
        U = SRUN[:, 256:384]
        sc = p.mset(S, 0.0, waits=[ph1v, ph1s], inc=1)
        do_scan = bool(CONV_PH & 2)
        Lr, Li = Lr_t[g], Li_t[g]
        pst2 = PSTT[:, :].rearrange("p (s q) -> p s q", q=128)
        sst2 = SSTB[:, :].rearrange("p (s q) -> p s q", q=128)
        sw = sc
        for s in range(CC if (CONV_PH & 2) else 0):
            p.cp(sst2[:, s, :], S, waits=[sw])
            p.tt(T, S, Lr, ALU.mult)
            p.tt(U[0:64, :], S[64:128, :], Li[64:128, :], ALU.mult)
            w2 = p.tt(U[64:128, :], S[0:64, :], Li[0:64, :], ALU.mult, inc=1)
            p.tt(S[0:64, :], T[0:64, :], U[0:64, :], ALU.subtract, waits=[w2])
            w3 = p.tt(S[64:128, :], T[64:128, :], U[64:128, :], ALU.add, inc=1)
            sw = p.tt(S, S, pst2[:, s, :], ALU.add, waits=[w3], inc=1)
        if g == 0 and globals().get("_TAP_SSTB") is not None:
            tdm2 = None
            for ck in range(32):
                cc1 = p.cp(SRUN[:, 0:256], SSTB[:, ck * 256:(ck + 1) * 256],
                           inc=1, waits=[tdm2] if tdm2 else ())
                tdm2 = p.dma(globals()["_TAP_SSTB"][:, ck * 256:(ck + 1) * 256],
                             SRUN[:, 0:256], waits=[cc1])
        scan_done = p.mark("vector")

        for wave in range(4 if (CONV_PH & 4) else 0):
            ch0 = wave * 64
            dglob0 = g * 256 + ch0
            sub = ch0 // 128
            esb = ES2[wave % 2]
            wlist = [scan_done]
            if es_free[wave % 2] is not None:
                wlist.append(es_free[wave % 2])
            ms = p.mset(esb[:], 0.0, waits=wlist, inc=1)
            es3 = esb[:, :].rearrange("p (c r) -> p c r", r=64)
            e1 = p.dma(es3[sub * 32:sub * 32 + 32, :, :],
                       apc(ebufh, dglob0 * 2 * NST * CC,
                           [(CC, 32), (2 * NST * CC, 64), (1, 64)]), waits=[ms])
            e2 = p.dma(es3[64 + sub * 32:64 + sub * 32 + 32, :, :],
                       apc(ebufh, dglob0 * 2 * NST * CC + NST * CC,
                           [(CC, 32), (2 * NST * CC, 64), (1, 64)]))
            ld = p.mark("sync")
            mm_marks = [None] * 8
            for c in range(64):
                ch = ch0 + c
                q = (ch0 % 128) + c
                b = c % 8
                blk = c // 8
                bank = pbank[b]
                w = [ld]
                if bank_free[b] is not None:
                    w.append(bank_free[b])
                out = bank[0:64, blk * 64:blk * 64 + 64]
                mv2 = SSTB[:, :].rearrange(
                    "p (c2 par q2) -> p par c2 q2", par=2, q2=128)[:, :, :, q]
                p.mm(out, es3[:, c, :], mv2, True, False, waits=w)
                mm_marks[b] = p.mm(out, identb[:, 0:64],
                                   YSTA[:, ch * 64:(ch + 1) * 64], False, True,
                                   inc=1, waits=[ph1s])
            for b in range(8):
                bank = pbank[b]
                ge = p.act(yst23[:, ch0 + b:ch0 + b + 57:8, :],
                           bank[0:64, 0:512].rearrange("p (k s) -> p k s", s=64),
                           AF.Gelu, waits=[mm_marks[b]], inc=1)
                bank_free[b] = ge
            es_free[wave % 2] = p.mark("tensor")
        ph2s = p.mark("scalar")

        for sub in range(2 if (CONV_PH & 8) else 0):
            for sblk in range(4):
                trm = None
                for si in range(16):
                    colpos = sblk * 16 + si
                    src = yst23[:, sub * 128:(sub + 1) * 128, colpos]
                    w = [ph2s] + ([ph3_prev] if ph3_prev else [])
                    if bank_free[0] is not None:
                        w.append(bank_free[0])
                    trm = p.tr(ptr[:, si * 64:(si + 1) * 64], src,
                               identb[0:64, 0:64], waits=w,
                               inc=(1 if si == 15 else 0))
                ev = p.cp(YBS[:], ptr[:],
                          waits=[trm] + ([ph3_dma] if ph3_dma else []), inc=1)
                tok0 = (sblk % 2) * 16 * 128 + (64 if sblk >= 2 else 0)
                ph3_dma = p.dma(
                    apc(ybh, (g * 256 + sub * 128) * L + tok0,
                        [(L, 128), (128, 16), (1, CC)]),
                    YBS[:, :].rearrange("p (si r) -> p si r", r=CC),
                    waits=[ev])
                ph3_prev = p.mark("sync")
                bank_free[0] = ev
        vb = p.mark("vector")
        for b in range(8):
            bank_free[b] = vb

    done = p.barrier()
    es.close()
    return done



def build_proj(nc, p, stack, dr, cst, ybh, x1h, zb, tap, tap_ap):
    es = contextlib.ExitStack()
    sbt = lambda n, s, d: es.enter_context(nc.sbuf_tensor(n, list(s), d))
    pst = lambda n, s, d: es.enter_context(nc.psum_tensor(n, list(s), d))

    identf = cst["identf"]
    identb = cst["identb"]
    ones128 = cst["ones128"]

    WL = sbt("WL", (128, D), F32)
    WBC = sbt("WBC", (128, D), BF16)
    WT = sbt("WT", (128, 8 * D), BF16)
    ptw = pst("ptw", (128, 512), BF16)
    brow = sbt("brow", (1, D), BF16)
    brf = sbt("brf", (1, D), F32)
    bm = p.dma(brf[:], dr["bssm"].rearrange("(a d) -> a d", a=1))
    p.cp(brow[:], brf[:], waits=[bm])
    wt3 = WT[:, :].rearrange("p (dc o) -> p dc o", o=D)
    wtm = None
    for oc in range(8):
        dm = p.dma(WL[:], dr["Wssm"][oc * 128:(oc + 1) * 128, :],
                   waits=[wtm] if wtm else ())
        cm = p.cp(WBC[:], WL[:], waits=[dm], inc=1)
        for grp in range(2):
            tms = None
            for k in range(4):
                dc = grp * 4 + k
                w = [cm] + ([wtm] if wtm else [])
                tms = p.tr(ptw[:, k * 128:(k + 1) * 128],
                           WBC[:, dc * 128:(dc + 1) * 128], identb,
                           waits=w, inc=(1 if k == 3 else 0))
            wtm = p.cp(wt3[:, grp * 4:grp * 4 + 4, oc * 128:(oc + 1) * 128],
                       ptw[:].rearrange("p (k o) -> p k o", o=128),
                       waits=[tms], inc=1)
    wt_done = p.barrier()[0]

    ZG = sbt("ZG", (128, 257), F32)
    zg = p.mset(ZG[:], 0.0, inc=1)
    for i in range(2):
        for dc in range(8):
            p.dma(apc(zb[i], dc * 128 * GW, [(GW, 128), (1, 257)]), ZG[:],
                  waits=[zg])

    YBT = sbt("YBT", (128, 8 * 128), BF16)
    XT = sbt("XT", (128, D), F32)
    X1T = sbt("X1T", (128, D), F32)
    VTf = sbt("VTf", (128, D), F32)
    Z0S = sbt("Z0S", (128, 8 * 128), F32)
    STT2 = sbt("STT2", (128, 24), F32)
    SC2 = sbt("SC2", (128, D), BF16)
    pa = [pst(f"pa{i}", (128, 512), F32) for i in range(2)]
    pz = [pst(f"pz{i}", (128, 512), F32) for i in range(2)]

    prevv = None
    x1m_prev = None
    z0d_prev = None
    for tc in range(32):
        wl = [wt_done] + ([prevv] if prevv else [])
        ybm = p.dma(YBT[:], apc(ybh, tc * 128, [(L, 128), (128 * L, 8), (1, 128)]),
                    waits=wl)
        xtm = p.dma(XT[:], dr["x"][tc * 128:(tc + 1) * 128, :])
        ld = p.mark("sync")
        mmk = []
        for half in range(2):
            for dc in range(8):
                p.mm(pa[half][:], YBT[:, dc * 128:(dc + 1) * 128],
                     WT[:, dc * D + half * 512: dc * D + half * 512 + 512],
                     dc == 0, False, waits=[ld] + ([prevv] if prevv else []))
            mmk.append(p.mm(pa[half][:], ones128[0:1, 0:128],
                            brow[:, half * 512:(half + 1) * 512], False, True,
                            inc=1))
        sx, sxx = STT2[:, 0:1], STT2[:, 1:2]
        mu, rs, ep = STT2[:, 2:3], STT2[:, 3:4], STT2[:, 4:5]
        var_ = STT2[:, 5:6]
        h1 = p.tt(X1T[:, 0:512], pa[0][:], XT[:, 0:512], ALU.add,
                  waits=[mmk[0], xtm] + ([x1m_prev] if x1m_prev else []))
        h2 = p.tt(X1T[:, 512:1024], pa[1][:], XT[:, 512:1024], ALU.add,
                  waits=[mmk[1]], inc=1)
        x1m = p.dma(x1h[tc * 128:(tc + 1) * 128, :], X1T[:], waits=[h2])
        x1m_prev = x1m
        m1 = p.ts(SC2[:, 0:D], X1T[:], 1.0, ALU.mult, s2=None, op1=ALU.add,
                  accum_out=sx, inc=1)
        m2 = p.act(SC2[:, 0:D], X1T[:], AF.Square, accum_out=sxx, waits=[h2], inc=1)
        lns = STT2[:, 8:16]
        lns2 = STT2[:, 16:24]
        p.ts(mu, sx, 1.0 / D, ALU.mult, waits=[m1, m2])
        w0 = p.ts(ep, sxx, 1.0 / D, ALU.mult, inc=1)
        p.stt(var_, mu, mu, ep, ALU.mult, ALU.subtract, waits=[w0], inc=1)
        w1 = p.mark("vector")
        for _k in range(8):
            p.ts(STT2[:, 8 + _k:9 + _k], var_, -1.0, ALU.mult, s2=1e-5,
                 op1=ALU.add, waits=[w1])
        vr = p.mark("vector")
        a1 = p.act(lns, lns, AF.Ln, waits=[vr], inc=1)
        a2 = p.act(lns2, lns, AF.Exp, scale=-0.5, waits=[a1], inc=1)
        rs = STT2[:, 16:17]
        vm = p.ts(VTf[:], X1T[:], mu, ALU.subtract, s2=rs, op1=ALU.mult,
                  waits=[a2], inc=1)
        evs = []
        for grp in range(2):
            bsel = pz[grp]
            tms = None
            for k in range(4):
                dc = grp * 4 + k
                w = [vm] + (evs[-2:] if evs else [])
                tms = p.tr(bsel[:, k * 128:(k + 1) * 128],
                           VTf[:, dc * 128:(dc + 1) * 128], identf,
                           waits=w, inc=(1 if k == 3 else 0))
            evs.append(p.cp(Z0S[:, grp * 512:(grp + 1) * 512], bsel[:],
                            waits=[tms] + ([z0d_prev] if z0d_prev else []),
                            inc=1))
        prevv = evs[-1]
        z0d_prev = p.dma(apc(zb[0], 257 + tc * 128,
                             [(GW, 128), (128 * GW, 8), (1, 128)]),
                         Z0S[:].rearrange("p (dc t) -> p dc t", t=128),
                         waits=[prevv])
        if tap == "x1":
            p.dma(tap_ap[tc * 128:(tc + 1) * 128, :], X1T[:], waits=[h2])
        if tap == "yb":
            cm2 = p.cp(VTf[:], YBT[:].rearrange("p (dc t) -> p dc t", t=128),
                       waits=[ybm], inc=1)
            p.dma(apc(tap_ap, tc * 128, [(L, 128), (128 * L, 8), (1, 128)]),
                  VTf[:].rearrange("p (dc t) -> p dc t", t=128), waits=[cm2])
        if tap == "z0":
            p.dma(apc(tap_ap, tc * 128, [(L, 128), (128 * L, 8), (1, 128)]),
                  Z0S[:].rearrange("p (dc t) -> p dc t", t=128), waits=[prevv])
    done = p.barrier()
    es.close()




def build_layers(nc, p, stack, dr, cst, zb, n_layers, tap, tap_ap, w2s):
    es = contextlib.ExitStack()
    sbt = lambda n, s, d: es.enter_context(nc.sbuf_tensor(n, list(s), d))
    pst = lambda n, s, d: es.enter_context(nc.psum_tensor(n, list(s), d))

    identb = cst["identb"]
    maskt = cst["maskt"]
    maskt3 = cst["maskt3"]
    permq = cst["permq"]
    ones128 = cst["ones128"]

    fence = p.barrier()
    for ch in range(8):
        cst[f"bd16_{ch}"] = sbt(f"c2_bd16_{ch}", (128, 16), BF16)[:]
        p.dma(cst[f"bd16_{ch}"], dr[f"bd16_{ch}"])
        cst[f"bcp_{ch}"] = sbt(f"c2_bcp_{ch}", (16, 128), BF16)[:]
        p.dma(cst[f"bcp_{ch}"], dr[f"bcp_{ch}"])
    LN8V = sbt("LN8V", (128, 8), F32)
    ln8m = p.mset(LN8V[:], LN8, inc=1)
    pm_cst2 = p.mark("sync")

    ZL = sbt("ZL", (128, 8 * 769), F32)
    XB = sbt("XB", (128, 8 * 768), BF16)
    Z1T = sbt("Z1T", (128, 4 * 512), F32)
    VECS = sbt("VECS", (128, 72), F32)
    QB = sbt("QB", (128, 512), BF16)
    Q2 = sbt("Q2", (128, 512), BF16)
    T1 = sbt("T1", (128, 512), BF16)
    U1 = sbt("U1", (128, 512), BF16)

    ARENA = sbt("ARENA", (128, 74496), BF16)

    class _Carve:
        def __init__(self):
            self.off = 0

        def take(self, shape, dt):
            size = 2 if dt == F32 else 2 if dt == BF16 else 4
            esz = 4 if dt == F32 else 2
            self.off = (self.off + 3) & ~3
            off_el = self.off // esz
            v = ARENA[:].bitcast(dt) if dt == F32 else ARENA[:]
            out = v[0:shape[0], off_el:off_el + shape[1]]
            self.off += shape[1] * esz
            assert self.off <= 74496 * 2, f"arena overflow {self.off}"
            return out

    ca = _Carve()
    WQB = ca.take((128, 8 * DQKV), BF16)
    WOB = ca.take((128, 8 * D), BF16)
    KWIN = ca.take((128, 8 * 768), BF16)
    VWIN = ca.take((128, 6 * 1024), BF16)
    QF = ca.take((128, 8 * 512), BF16)
    KPRE = ca.take((128, 8 * 512), BF16)
    AO = ca.take((128, 8 * 512), BF16)
    PB = ca.take((128, 16 * 384), BF16)
    PTS = ca.take((128, 6 * 128), BF16)
    ROTT = ca.take((128, 4 * 512), BF16)
    RQS = ca.take((16, 512), F32)
    RQB = ca.take((16, 512), BF16)
    RKB = ca.take((16, 512), BF16)
    NMS = ca.take((128, 16), F32)
    DENS = ca.take((128, 16), F32)
    RDS = ca.take((128, 16), F32)

    cb = _Carve()
    W1B = cb.take((128, 8 * D1), BF16)
    W2BH0 = cb.take((128, 22 * 256), BF16)
    W2BH1 = cb.take((128, 22 * 256), BF16)
    GG = cb.take((128, 22 * 512), BF16)
    XN = cb.take((128, 8 * 512), BF16)
    SQT = cb.take((128, 512), BF16)
    MUB = cb.take((128, 512), F32)
    VARB = cb.take((128, 512), F32)
    RSB = cb.take((128, 512), F32)
    GE = cb.take((128, 512), BF16)

    pk = [pst(f"lps{i}", (128, 512), F32) for i in range(7)]
    ppt = pst("lppt", (128, 768), BF16)
    prn = pk[6][0:16, :]
    ppt2 = pk[3][:].bitcast(BF16)[:, 0:768]

    PTS2 = KPRE[:, 0:768]
    pm = {}

    qsv, ksv = VECS[:, 0:1], VECS[:, 1:2]
    qspv, kspv = VECS[:, 2:3], VECS[:, 3:4]

    for li in range(n_layers):
        cm = pm.get("wa_read")
        if cm is None and pm.get("zl_read"):
            cm = pm["zl_read"]
        if cm is None:
            cm = fence[0]
        WLDA = PB.bitcast(F32)
        LH = [WLDA[:, 0:1536], WLDA[:, 1536:3072]]
        cmv = [None, None]
        ci = 0
        for dc in range(8):
            for hf in range(2):
                sl = ci % 2
                w = [cmv[sl]] if cmv[sl] else [cm]
                ld = p.dma(LH[sl],
                           dr["Wqkv"][li, dc * 128:(dc + 1) * 128,
                                      hf * 1536:(hf + 1) * 1536],
                           eng="gpsimd", waits=w)
                cmv[sl] = p.cp(WQB[:, dc * DQKV + hf * 1536:
                                   dc * DQKV + (hf + 1) * 1536], LH[sl],
                               waits=[ld], inc=1,
                               eng="vector" if sl == 0 else "scalar")
                ci += 1
        for dc in range(8):
            sl = ci % 2
            ld = p.dma(LH[sl][:, 0:1024], dr["Wo"][li, dc * 128:(dc + 1) * 128, :],
                       eng="gpsimd", waits=[cmv[sl]])
            cmv[sl] = p.cp(WOB[:, dc * D:(dc + 1) * D], LH[sl][:, 0:1024],
                           waits=[ld], inc=1,
                           eng="vector" if sl == 0 else "scalar")
            ci += 1
        wa_cast = cmv[0]
        wa_cast2 = cmv[1]
        vw = [pm["v_read"]] if pm.get("v_read") else []
        p.dma(VECS[0:64, 0:1], apc(dr["q_scale"], li * DH, [(1, 64), (1, 1)]),
              allow_slow_non_contiguous=True, waits=vw)
        p.dma(VECS[64:128, 0:1], apc(dr["q_scale"], li * DH, [(1, 64), (1, 1)]),
              allow_slow_non_contiguous=True)
        p.dma(VECS[0:64, 1:2], apc(dr["k_scale"], li * DH, [(1, 64), (1, 1)]),
              allow_slow_non_contiguous=True)
        p.dma(VECS[64:128, 1:2], apc(dr["k_scale"], li * DH, [(1, 64), (1, 1)]),
              allow_slow_non_contiguous=True)
        for half in (0, 64):
            for (col, nm) in ((2, "q_scale"), (3, "k_scale")):
                p.dma(VECS[half:half + 32, col:col + 1],
                      apc(dr[nm], li * DH + 32, [(1, 32), (1, 1)]),
                      allow_slow_non_contiguous=True)
                p.dma(VECS[half + 32:half + 64, col:col + 1],
                      apc(dr[nm], li * DH, [(1, 32), (1, 1)]),
                      allow_slow_non_contiguous=True)
        p.dma(VECS[:, 4:12], apc(dr["ln_g"], li * D, [(1, 128), (128, 8)]),
              allow_slow_non_contiguous=True)
        p.dma(VECS[:, 12:20], apc(dr["ln_b"], li * D, [(1, 128), (128, 8)]),
              allow_slow_non_contiguous=True)
        p.dma(VECS[:, 20:28], apc(dr["b2"], li * D, [(1, 128), (128, 8)]),
              allow_slow_non_contiguous=True)
        for js in range(22):
            wcols = 128 if js < 21 else 42
            p.dma(VECS[0:wcols, 28 + js:29 + js],
                  apc(dr["b1"], li * D1 + js * 128, [(1, wcols), (1, 1)]),
                  allow_slow_non_contiguous=True)
            p.dma(VECS[0:wcols, 50 + js:51 + js],
                  apc(dr["b1"], li * D1 + DI + js * 128, [(1, wcols), (1, 1)]),
                  allow_slow_non_contiguous=True)
        wv_mark = p.mark("sync")

        zsrc, zdst = zb[0], zb[1]
        z1_dma = None
        for ti in range(NTILE):
            t0 = ti * TW
            zl3 = ZL[:, :].rearrange("p (dc c) -> p dc c", c=769)
            xb3 = XB[:, :].rearrange("p (dc c) -> p dc c", c=768)
            kw3 = KWIN.rearrange("p (h c) -> p h c", c=768)

            kroll = None
            if ti > 0:
                kroll = p.cp(kw3[:, :, 0:256], kw3[:, :, 512:768],
                             waits=[pm["qk_read"]], inc=1)
            rotw = [pm["rot_read"]] if pm.get("rot_read") else []
            rm0 = p.dma(ROTT[:, 0:512], dr["csq"][:, t0:t0 + 512], waits=rotw)
            p.dma(ROTT[:, 512:1024], dr["snq"][:, t0:t0 + 512])
            p.dma(ROTT[:, 1024:1536], dr["csk"][:, t0:t0 + 512])
            p.dma(ROTT[:, 1536:2048], dr["snk"][:, t0:t0 + 512])
            wl = [wa_cast, wa_cast2, wv_mark]
            if pm.get("zl_read"):
                wl.append(pm["zl_read"])
            if z1_dma:
                wl.append(z1_dma)
            rot_m = p.mark("sync")
            zl_m0 = p.dma(ZL[:, 0:4 * 769],
                          apc(zsrc, t0, [(GW, 128), (128 * GW, 4), (1, 769)]),
                          waits=wl)
            zl_m = p.dma(ZL[:, 4 * 769:8 * 769],
                         apc(zsrc, t0 + 4 * 128 * GW,
                             [(GW, 128), (128 * GW, 4), (1, 769)]))
            xb_m0 = p.cp(xb3[:, 0:4, :], zl3[:, 0:4, 1:769],
                         waits=[zl_m0, pm.get("xb_read")], inc=1)
            xb_m = p.cp(xb3[:, 4:8, :], zl3[:, 4:8, 0:768], waits=[zl_m],
                        inc=1)

            fold_marks = {}
            for kind in range(2):
                DST = QF if kind == 0 else KPRE
                sv = qsv if kind == 0 else ksv
                spv = qspv if kind == 0 else kspv
                CS = ROTT[:, 0:512] if kind == 0 else ROTT[:, 1024:1536]
                SN = ROTT[:, 512:1024] if kind == 0 else ROTT[:, 1536:2048]
                pre_marks = []
                for hp in range(8):
                    oc = kind * 8 + hp
                    bank = pk[[0, 1, 4, 5][oc % 4]]
                    w = [xb_m0, rot_m]
                    for k2 in (f"qkb{oc % 4}", f"sc{oc % 4}"):
                        if pm.get(k2):
                            w.append(pm[k2])
                    for dc in range(8):
                        p.mm(bank[:], WQB[:, dc * DQKV + oc * 128:
                                          dc * DQKV + (oc + 1) * 128],
                             xb3[:, dc, 256:768], dc == 0, dc == 7,
                             waits=w if dc == 0 else
                             ([xb_m] if dc == 4 else ()))
                    mmq = p.mark("tensor")
                    e1 = p.cp(QB[:], bank[:], waits=[mmq, pm.get("qb_read")],
                              inc=1, eng="scalar")
                    e2 = p.act(Q2[:], QB[:], AF.Square, waits=[e1], inc=1)
                    p.mm(prn, cst[f"bd16_{hp}"], Q2[:], hp == 0, hp == 7,
                         waits=[e2, pm_cst2] + ([pm[k2] for k2 in
                                                 ("rn_read", "avb0")
                                        if (hp == 0 and pm.get(k2))]))
                    pmm = p.mm(pk[2][:], permq, QB[:], True, True,
                               waits=[e1] + ([pm["perm_read"]] if
                                             pm.get("perm_read") else []), inc=1)
                    p.stt(T1[:], QB[:], sv, CS, ALU.mult, ALU.mult,
                          waits=[e1, pm.get("t1_read")])
                    u1m = p.stt(U1[:], pk[2][:], spv, SN, ALU.mult, ALU.mult,
                                waits=[pmm, pm.get("t1_read")], inc=1)
                    pm["perm_read"] = u1m
                    r3 = p.tt(DST[:, hp * 512:(hp + 1) * 512], T1[:], U1[:],
                              ALU.add, waits=[u1m] +
                              [pm[k] for k in
                               (("qf_read",) if (hp == 0 and kind == 0) else
                                ("pts_read1",) if (hp == 0 and kind == 1) else
                                ()) if pm.get(k)], inc=1)
                    pm["t1_read"] = r3
                    pm["qb_read"] = r3
                    pm[f"qkb{oc % 4}"] = r3
                    pre_marks.append(r3)
                rn_mm = p.mark("tensor")
                rv = p.ts(RQS, prn[:], 1e-24, ALU.add,
                          waits=[rn_mm, pm.get("rqs_read")], inc=1)
                a1 = p.act(RQS, RQS, AF.Ln, waits=[rv, ln8m], inc=1)
                RB = RQB if kind == 0 else RKB
                a2 = p.act(RB[:], RQS, AF.Exp, scale=-0.5,
                           bias=(LN8V[0:16, 0:1] if kind == 0 else 0.0),
                           waits=[a1, pm.get("rb_read")], inc=1)
                pm["rqs_read"] = a2
                pm["rn_read"] = a2
                for hp in range(8):
                    bm = p.mm(pk[3][:], cst[f"bcp_{hp}"], RB[:], True, True,
                              waits=[a2, pm_cst2] +
                              [pm[k] for k in ("bcb", "pk3_attn")
                               if pm.get(k)], inc=1)
                    dst_sl = DST[:, hp * 512:(hp + 1) * 512]
                    fold = p.tt(dst_sl, dst_sl, pk[3][:], ALU.mult,
                                waits=[bm, pre_marks[hp]], inc=1)
                    pm["bcb"] = fold
                    if kind == 1:
                        p.cp(kw3[:, hp, 256:768], dst_sl,
                             waits=[fold] + ([kroll] if kroll else []), inc=1)
                pm["rb_read"] = p.mark("vector")
            qf_done = p.mark("vector")
            if li == 0 and ti == 0 and globals().get("_TAP_A") in ("qf", "kw"):
                src = QF if globals()["_TAP_A"] == "qf" else KWIN[:, 0:4096]
                bar = p.barrier()
                tdm = None
                for ck in range(8):
                    cc1 = p.cp(Z1T[:, 0:512], src[:, ck * 512:(ck + 1) * 512],
                               inc=1, waits=[bar[0]] + ([tdm] if tdm else []))
                    tdm = p.dma(globals()["_TAP_AAP"][:, ck * 512:(ck + 1) * 512],
                                Z1T[:, 0:512], waits=[cc1])
                p.barrier()

            vt_marks = []
            for tb in range(4):
                slot = (ti * 4 + tb + 2) % 6
                w = [xb_m0, qf_done]
                for key in ("qkb0", "qkb1", "sc2", "sc3", f"vwin_read_{slot}"):
                    if pm.get(key):
                        w.append(pm[key])
                for dc in range(8):
                    st = xb3[:, dc, 256 + tb * 128:256 + (tb + 1) * 128]
                    p.mm(pk[4][:], st, WQB[:, dc * DQKV + 2048:dc * DQKV + 2560],
                         dc == 0, dc == 7, waits=w if dc == 0 else
                         ([xb_m] if dc == 4 else ()))
                    p.mm(pk[5][:], st, WQB[:, dc * DQKV + 2560:dc * DQKV + 3072],
                         dc == 0, dc == 7)
                mmv = p.mark("tensor")
                p.cp(VWIN[:, slot * 1024:slot * 1024 + 512], pk[4][:],
                     waits=[mmv], inc=1, eng="scalar")
                v2 = p.cp(VWIN[:, slot * 1024 + 512:(slot + 1) * 1024], pk[5][:],
                          inc=1, eng="scalar")
                vt_marks.append(v2)
                pm["qkb0"] = v2
                pm["qkb1"] = v2
            vt_done = vt_marks[-1]

            ao_marks = []
            avm = None
            qkbanks = [pk[0], pk[1], pk[4], pk[5]]
            for qc in range(4):
                qcg = ti * 4 + qc
                nj = min(3, qcg + 1)
                wwin = nj * 128
                kcol0 = 256 if qcg <= 1 else qc * 128
                exp_marks = []
                for h in range(16):
                    hp, hh = h // 2, h % 2
                    base = 64 * hh
                    sb_ = qkbanks[h % 4]
                    w = [qf_done, vt_done]
                    if pm.get(f"sc{h % 4}"):
                        w.append(pm[f"sc{h % 4}"])
                    qst = QF[base:base + 64,
                             hp * 512 + qc * 128:hp * 512 + (qc + 1) * 128]
                    kmv = KWIN[base:base + 64,
                               hp * 768 + kcol0:hp * 768 + kcol0 + wwin]
                    p.mm(sb_[:, 0:wwin], qst, kmv, True, False, waits=w,
                         tile_position=(base, 0))
                    if qcg == 0:
                        p.mm(sb_[:, 0:128], identb, maskt[:, 128:256],
                             False, True)
                    elif nj == 2:
                        p.mm(sb_[:, wwin - 128:wwin], identb,
                             maskt[:, 128:256], False, True)
                    else:
                        p.mm(sb_[:, 0:384], identb, maskt3, False, True)
                    msc = p.mark("tensor")
                    rdm = p.red(NMS[:, h:h + 1], sb_[:, 0:wwin], op=ALU.max,
                                negate=True, waits=[msc])
                    ex = p.act(PB[:, h * 384:h * 384 + wwin], sb_[:, 0:wwin],
                               AF.Exp, bias=NMS[:, h:h + 1],
                               accum_out=DENS[:, h:h + 1],
                               waits=[rdm, pm.get("pb_read")], inc=1)
                    exp_marks.append(ex)
                    pm[f"sc{h % 4}"] = ex
                rdv = p.op("vector",
                           lambda v: v.reciprocal(out=RDS, in_=DENS),
                           waits=exp_marks + ([pm["rds_read"]] if
                                              pm.get("rds_read") else []), inc=1)
                for hp in range(8):
                    par = hp % 2
                    PPT = ppt if par == 0 else ppt2
                    PTSb = PTS if par == 0 else PTS2
                    trm = None
                    for hh in range(2):
                        h = 2 * hp + hh
                        nm2 = p.ts(PB[:, h * 384:h * 384 + wwin],
                                   PB[:, h * 384:h * 384 + wwin],
                                   RDS[:, h:h + 1], ALU.mult, waits=[rdv], inc=1)
                        for c in range(nj):
                            w2_ = [nm2]
                            if pm.get(f"ppt_read{par}"):
                                w2_.append(pm[f"ppt_read{par}"])
                            trm = p.tr(PPT[:, (hh * 3 + c) * 128:
                                           (hh * 3 + c) * 128 + 128],
                                       PB[:, h * 384 + c * 128:
                                          h * 384 + (c + 1) * 128],
                                       identb, waits=w2_, inc=1)
                    pte = p.cp(PTSb[:, 0:768], PPT[:, 0:768],
                               waits=[trm] + ([pm[f"pts_read{par}"]] if
                                              pm.get(f"pts_read{par}") else []),
                               inc=1)
                    pm[f"ppt_read{par}"] = pte
                    if par == 1:
                        pm["pk3_attn"] = pte
                    avbank = pk[6] if hp < 4 else pk[2]
                    reg = (hp % 4) * 128
                    w = [pte]
                    if pm.get(f"avb{hp // 4}"):
                        w.append(pm[f"avb{hp // 4}"])
                    for c in range(nj):
                        jblk = qcg - (nj - 1) + c
                        slot = (jblk + 2) % 6
                        for hh in range(2):
                            h = 2 * hp + hh
                            p.mm(avbank[64 * hh:64 * hh + 64, reg:reg + 128],
                                 VWIN[:, slot * 1024 + h * 64:
                                      slot * 1024 + (h + 1) * 64],
                                 PTSb[:, (hh * 3 + c) * 128:
                                      (hh * 3 + c) * 128 + 128],
                                 c == 0, False,
                                 waits=w if (c == 0 and hh == 0) else (),
                                 tile_position=(0, 64 * hh))
                    avm = p.mark("tensor")
                    pm[f"pts_read{par}"] = avm
                    for c in range(nj):
                        pm[f"vwin_read_{(qcg - (nj - 1) + c + 2) % 6}"] = avm
                pm["rds_read"] = avm
                pm["pb_read"] = avm
                ev0 = p.op("scalar", lambda s2, qc=qc: s2.copy(
                    out=AO.rearrange("p (hp c) -> p hp c", c=512)
                    [:, 0:4, qc * 128:(qc + 1) * 128],
                    in_=pk[6][:].rearrange("p (r c) -> p r c", c=128)),
                    waits=[avm] + ([pm["ao_read"]] if pm.get("ao_read") else []),
                    inc=1)
                ev1 = p.op("scalar", lambda s2, qc=qc: s2.copy(
                    out=AO.rearrange("p (hp c) -> p hp c", c=512)
                    [:, 4:8, qc * 128:(qc + 1) * 128],
                    in_=pk[2][:].rearrange("p (r c) -> p r c", c=128)),
                    inc=1)
                pm["avb0"] = ev0
                pm["avb1"] = ev1
                ao_marks.append(ev1)
                if (li == 0 and ti == 0 and qc == 0 and
                        globals().get("_TAP_A") == "ao0"):
                    bar = p.barrier()
                    tdm = None
                    for ck in range(8):
                        cc1 = p.cp(Z1T[:, 0:512], AO[:, ck * 512:(ck + 1) * 512],
                                   inc=1, waits=[bar[0]] + ([tdm] if tdm else []))
                        tdm = p.dma(globals()["_TAP_AAP"][:, ck * 512:(ck + 1) * 512],
                                    Z1T[:, 0:512], waits=[cc1])
                    p.barrier()
            pm["qk_read"] = p.mark("tensor")
            pm["qf_read"] = pm["qk_read"]
            pm["rot_read"] = pm["qk_read"]
            pm["perm_read"] = pm["avb1"]
            if li == 0 and ti == 0 and globals().get("_TAP_A") == "vw":
                bar = p.barrier()
                tdm = None
                for ck in range(8):
                    cc1 = p.cp(Z1T[:, 0:512],
                               VWIN[:, 2048 + ck * 512:2048 + (ck + 1) * 512],
                               inc=1, waits=[bar[0]] + ([tdm] if tdm else []))
                    tdm = p.dma(globals()["_TAP_AAP"][:, ck * 512:(ck + 1) * 512],
                                Z1T[:, 0:512], waits=[cc1])
                p.barrier()
            if li == 0 and ti == 0 and globals().get("_TAP_A") == "ao":
                bar = p.barrier()
                tdm = None
                for ck in range(8):
                    cc1 = p.cp(Z1T[:, 0:512], AO[:, ck * 512:(ck + 1) * 512],
                               inc=1, waits=[bar[0]] + ([tdm] if tdm else []))
                    tdm = p.dma(globals()["_TAP_AAP"][:, ck * 512:(ck + 1) * 512],
                                Z1T[:, 0:512], waits=[cc1])
                p.barrier()

            z1m = []
            h0 = None
            for oc in range(8):
                bank = pk[oc % 2]
                w = ao_marks + [pm[k2] for k2 in (f"qkb{oc % 2}", f"sc{oc % 2}")
                                if pm.get(k2)]
                for dc in range(8):
                    p.mm(bank[:], WOB[:, dc * D + oc * 128:dc * D + (oc + 1) * 128],
                         AO[:, dc * 512:(dc + 1) * 512], dc == 0, dc == 7,
                         waits=w if dc == 0 else ())
                mwo = p.mark("tensor")
                zw = [mwo]
                if oc == 0 and z1_dma:
                    zw.append(z1_dma)
                if oc == 4:
                    zw.append(h0)
                zz = p.tt(Z1T[:, (oc % 4) * 512:(oc % 4 + 1) * 512], bank[:],
                          zl3[:, oc, 257:769], ALU.add, waits=zw, inc=1)
                pm[f"qkb{oc % 2}"] = zz
                z1m.append(zz)
                if oc == 3:
                    h0 = p.dma(apc(zdst, 257 + t0,
                                   [(GW, 128), (128 * GW, 4), (1, 512)]),
                               Z1T[:].rearrange("p (dc c) -> p dc c", c=512),
                               waits=z1m[0:4])
            tmk = p.mark("tensor")
            pm["xb_read"] = tmk
            pm["ao_read"] = tmk
            pm["zl_read"] = z1m[-1]
            pm["v_read"] = z1m[-1]
            z1_dma = p.dma(apc(zdst, 257 + t0 + 4 * 128 * GW,
                               [(GW, 128), (128 * GW, 4), (1, 512)]),
                           Z1T[:].rearrange("p (dc c) -> p dc c", c=512),
                           waits=z1m[4:8])
        pm["za_done"] = z1_dma

        if tap == "z" and globals().get("_TAP_Z") == (li, "a"):
            bar = p.barrier()
            p.dma(tap_ap, apc(zb[1], 257, [(GW, D), (1, L)]), waits=[bar[0]])

        WLDB = GG.bitcast(F32)
        LB = [WLDB[:, 0:2730], WLDB[:, 2730:5460]]
        bguard = [pm["xb_read"], pm["za_done"], pm["zl_read"]]
        cmb = [None, None]
        ci = 0
        for dc in range(8):
            for hf in range(2):
                sl = ci % 2
                w = [cmb[sl]] if cmb[sl] else bguard
                ld = p.dma(LB[sl][:, 0:2730],
                           dr["W1"][li, dc * 128:(dc + 1) * 128,
                                    hf * 2730:(hf + 1) * 2730],
                           eng="gpsimd", waits=w)
                cmb[sl] = p.cp(W1B[:, dc * D1 + hf * 2730:
                                   dc * D1 + (hf + 1) * 2730],
                               LB[sl][:, 0:2730], waits=[ld], inc=1,
                               eng="vector" if sl == 0 else "scalar")
                ci += 1
        wb_cast = cmb[0]
        wb_cast2 = cmb[1]
        stb = [None, None]
        for ks in range(22):
            rows = 128 if ks < 21 else DI - 21 * 128
            sl = ks % 2
            ld = p.dma(LB[sl][0:rows, 0:1024],
                       dr["W2"][li, ks * 128:ks * 128 + rows, :],
                       eng="gpsimd", waits=[cmb[sl]])
            WH2 = W2BH0 if sl == 0 else W2BH1
            cw = [ld]
            if stb[sl]:
                cw.append(stb[sl])
            else:
                cw += bguard + [pm[k] for k in (f"w2h_read{sl}",) if pm.get(k)]
            cmb[sl] = p.cp(WH2[0:rows, 0:1024], LB[sl][0:rows, 0:1024],
                           waits=cw, inc=1, eng="scalar")
            stb[sl] = p.dma(apc(w2s, ks * 128 * D, [(D, rows), (1, D)]),
                            WH2[0:rows, 0:1024], waits=[cmb[sl]], eng="gpsimd")
        wb_st = [stb[0], stb[1]]
        pm["gg_read"] = cmb[1]

        zsrc, zdst = zb[1], zb[0]
        z2_dma = None
        for ti in range(NTILE):
            t0 = ti * TW
            xn3 = XN.rearrange("p (dc c) -> p dc c", c=512)
            wl = [pm["za_done"], wb_cast, wb_cast2]
            if pm.get("zl_read"):
                wl.append(pm["zl_read"])
            if z2_dma:
                wl.append(z2_dma)
            zl_m0 = p.dma(ZL[:, 0:4 * 513], apc(zsrc, 256 + t0,
                          [(GW, 128), (128 * GW, 4), (1, 513)]), waits=wl)
            zl_m = p.dma(ZL[:, 4 * 513:8 * 513],
                         apc(zsrc, 256 + t0 + 4 * 128 * GW,
                             [(GW, 128), (128 * GW, 4), (1, 513)]))
            zlb = ZL[:, 0:8 * 513].rearrange("p (dc c) -> p dc c", c=513)
            xsb3 = XB[:, 0:8 * 512].rearrange("p (dc c) -> p dc c", c=512)
            cpa = p.cp(xsb3[:, 0:4, :], zlb[:, 0:4, 1:513],
                       waits=[zl_m0, pm.get("xb_read")], inc=1)
            xs_m = p.cp(xsb3[:, 4:8, :], zlb[:, 4:8, 0:512], waits=[zl_m],
                        inc=1)
            w = [cpa] + [pm[k] for k in ("lnps_read", "w2b1") if pm.get(k)]
            for dc in range(8):
                p.mm(pk[0][:], ones128, xsb3[:, dc, :], dc == 0, dc == 7,
                     waits=w if dc == 0 else ([xs_m] if dc == 4 else ()))
            sqw = [cpa] + ([pm["lnps_read"]] if pm.get("lnps_read") else [])
            for dc in range(8):
                sm = p.tt(SQT, xsb3[:, dc, :], xsb3[:, dc, :], ALU.mult,
                          waits=(sqw if dc == 0 else
                                 ([xs_m] if dc == 4 else [])) +
                                ([pm["sqt_read"]] if pm.get("sqt_read") else []),
                          inc=1)
                pm["sqt_read"] = p.mm(pk[1][:], ones128, SQT, dc == 0, dc == 7,
                                      waits=[sm], inc=1)
            mst = p.mark("tensor")
            p.ts(MUB, pk[0][:], 1.0 / D, ALU.mult,
                 waits=[mst] + ([pm["mub_read"]] if pm.get("mub_read") else []))
            v0 = p.ts(VARB, pk[1][:], 1.0 / D, ALU.mult, inc=1)
            m2m = p.tt(pk[2][:], MUB, MUB, ALU.mult,
                       waits=[v0] + ([pm["w1b0"]] if pm.get("w1b0") else []),
                       inc=1)
            vv = p.tt(RSB, VARB, pk[2][:], ALU.subtract, waits=[m2m], inc=1)
            vv2 = p.ts(RSB, RSB, 1e-5, ALU.add, waits=[vv], inc=1)
            b1a = p.act(RSB, RSB, AF.Ln, waits=[vv2], inc=1)
            b2a = p.act(RSB, RSB, AF.Exp, scale=-0.5, waits=[b1a], inc=1)
            pm["lnps_read"] = b2a
            xn_m = None
            for dc in range(8):
                p.tt(T1[:], xsb3[:, dc, :], MUB, ALU.subtract,
                     waits=[b2a, pm.get("t1_read")])
                tm = p.mark("vector")
                p.tt(U1[:], T1[:], RSB, ALU.mult, waits=[tm])
                tm2 = p.mark("vector")
                pm["t1_read"] = tm2
                xn_m = p.ts(xn3[:, dc, :], U1[:], VECS[:, 4 + dc:5 + dc],
                            ALU.mult, s2=VECS[:, 12 + dc:13 + dc],
                            op1=ALU.add, waits=[tm2] +
                            ([pm["xn_read"]] if (dc == 0 and pm.get("xn_read"))
                             else []), inc=1)
            pm["mub_read"] = xn_m
            if (li == 0 and ti == 0 and
                    globals().get("_TAP_A") == "xn"):
                bar = p.barrier()
                tdm = None
                for ck in range(8):
                    cc1 = p.cp(Z1T[:, 0:512], XN[:, ck * 512:(ck + 1) * 512],
                               inc=1, waits=[bar[0]] + ([tdm] if tdm else []))
                    tdm = p.dma(globals()["_TAP_AAP"][:, ck * 512:(ck + 1) * 512],
                                Z1T[:, 0:512], waits=[cc1])
                p.barrier()

            gg_m = []
            for ks in range(22):
                wcols = 128 if ks < 21 else 42
                aoff = ks * 128
                goff = DI + ks * 128
                pa = pk[2 + (ks % 2) * 2]
                pg = pk[3 + (ks % 2) * 2]
                w = [xn_m]
                if pm.get(f"w1b{ks % 2}"):
                    w.append(pm[f"w1b{ks % 2}"])
                for dc in range(8):
                    p.mm(pa[0:wcols, :], W1B[:, dc * D1 + aoff:dc * D1 + aoff + wcols],
                         xn3[:, dc, :], dc == 0, dc == 7,
                         waits=w if dc == 0 else ())
                for dc in range(8):
                    p.mm(pg[0:wcols, :], W1B[:, dc * D1 + goff:dc * D1 + goff + wcols],
                         xn3[:, dc, :], dc == 0, dc == 7)
                mw1 = p.mark("tensor")
                ge_m = p.act(GE[0:wcols, :], pg[0:wcols, :], AF.Gelu,
                             bias=VECS[0:wcols, 50 + ks:51 + ks],
                             waits=[mw1] + ([pm["ge_read"]] if pm.get("ge_read")
                                            else []), inc=1)
                ggm = p.stt(GG[0:wcols, ks * 512:(ks + 1) * 512], pa[0:wcols, :],
                            VECS[0:wcols, 28 + ks:29 + ks], GE[0:wcols, :],
                            ALU.add, ALU.mult,
                            waits=[ge_m] + ([pm["gg_read"]] if
                                            (ks == 0 and pm.get("gg_read"))
                                            else []), inc=1)
                pm["ge_read"] = ggm
                pm[f"w1b{ks % 2}"] = ggm
                gg_m.append(ggm)
            if (li == 0 and ti == 0 and
                    globals().get("_TAP_A") == "gg"):
                bar = p.barrier()
                tdm = None
                for ck in range(22):
                    cc1 = p.cp(Z1T[:, 0:512], GG[:, ck * 512:(ck + 1) * 512],
                               inc=1, waits=[bar[0]] + ([tdm] if tdm else []))
                    tdm = p.dma(globals()["_TAP_AAP"][:, ck * 512:(ck + 1) * 512],
                                Z1T[:, 0:512], waits=[cc1])
                p.barrier()
            z2m = []
            qmm = {}
            hld = {}

            def w2qload(q, extra):
                WH = W2BH0 if q % 2 == 0 else W2BH1
                w = wb_st + [e for e in extra if e]
                p.dma(WH.rearrange("p (ks c) -> p ks c", c=256)[0:128, 0:21, :],
                      apc(w2s, q * 256,
                          [(D, 128), (128 * D, 21), (1, 256)]), waits=w)
                p.dma(WH[0:42, :].rearrange("p (ks c) -> p ks c", c=256)
                      [:, 21:22, :],
                      apc(w2s, 21 * 128 * D + q * 256,
                          [(D, 42), (1, 256)]))
                hld[q] = p.mark("sync")
            for q in range(2):
                w2qload(q, [pm.get(f"w2h_read{q % 2}")])
            for q in range(4):
                WH = W2BH0 if q % 2 == 0 else W2BH1
                for och in range(2):
                    oc = q * 2 + och
                    bank = pk[6] if oc % 2 == 0 else pk[0]
                    w = [gg_m[-1], hld[q]]
                    if pm.get(f"w2b{oc % 2}"):
                        w.append(pm[f"w2b{oc % 2}"])
                    for ks in range(22):
                        rows = 128 if ks < 21 else 42
                        p.mm(bank[:], WH[0:rows, ks * 256 + och * 128:
                                         ks * 256 + och * 128 + 128],
                             GG[0:rows, ks * 512:(ks + 1) * 512], ks == 0, ks == 21,
                             waits=w if ks == 0 else ())
                    mw2 = p.mark("tensor")
                    if och == 1:
                        qmm[q] = mw2
                        pm[f"w2h_read{q % 2}"] = mw2
                        if q + 2 <= 3:
                            w2qload(q + 2, [mw2])
                    zw = [mw2]
                    if oc == 0 and z2_dma:
                        zw.append(z2_dma)
                    if oc == 4:
                        zw.append(h0)
                    zz = p.stt(Z1T[:, (oc % 4) * 512:(oc % 4 + 1) * 512], bank[:],
                               VECS[:, 20 + oc:21 + oc],
                               zlb[:, oc, 1:513], ALU.add, ALU.add,
                               waits=zw, inc=1)
                    pm[f"w2b{oc % 2}"] = zz
                    z2m.append(zz)
                    if oc == 3:
                        h0 = p.dma(apc(zdst, 257 + t0,
                                       [(GW, 128), (128 * GW, 4), (1, 512)]),
                                   Z1T[:].rearrange("p (dc c) -> p dc c", c=512),
                                   waits=z2m[0:4])
            tmk = p.mark("tensor")
            pm["xb_read"] = tmk
            pm["gg_read"] = tmk
            pm["w2h_read"] = tmk
            pm["xn_read"] = tmk
            pm["zl_read"] = z2m[-1]
            z2_dma = p.dma(apc(zdst, 257 + t0 + 4 * 128 * GW,
                               [(GW, 128), (128 * GW, 4), (1, 512)]),
                           Z1T[:].rearrange("p (dc c) -> p dc c", c=512),
                           waits=z2m[4:8])
        pm["za_done"] = z2_dma
        pm["wa_read"] = pm["xb_read"]
        pm["v_read"] = z2m[-1]

        if tap == "z" and globals().get("_TAP_Z") == (li, "b"):
            bar = p.barrier()
            p.dma(tap_ap, apc(zb[0], 257, [(GW, D), (1, L)]), waits=[bar[0]])

    p.barrier()
    es.close()



def build_final(nc, p, stack, dr, cst, zb, x1h, out, n_layers):
    es = contextlib.ExitStack()
    sbt = lambda n, s, d: es.enter_context(nc.sbuf_tensor(n, list(s), d))
    pst = lambda n, s, d: es.enter_context(nc.psum_tensor(n, list(s), d))
    identf = cst["identf"]
    ZT = sbt("fZT", (128, 8 * 128), F32)
    X1L = sbt("fX1", (128, D), F32)
    OT = sbt("fOT", (128, D), F32)
    pzf = [pst(f"fpz{i}", (128, 512), F32) for i in range(2)]
    zsrc = zb[0]
    prevv = None
    st_marks = []
    for tc in range(32):
        wl = [prevv] if prevv else ()
        zm = p.dma(ZT[:], apc(zsrc, 257 + tc * 128,
                              [(GW, 128), (128 * GW, 8), (1, 128)]), waits=wl)
        xm = p.dma(X1L[:], x1h[tc * 128:(tc + 1) * 128, :])
        ld = p.mark("sync")
        evs = []
        for grp in range(2):
            bsel = pzf[grp]
            tms = None
            for k in range(4):
                dc = grp * 4 + k
                w = [ld] + (evs[-2:] if evs else [])
                tms = p.tr(bsel[:, k * 128:(k + 1) * 128],
                           ZT[:, dc * 128:(dc + 1) * 128], identf,
                           waits=w, inc=(1 if k == 3 else 0))
            evs.append(p.tt(OT[:, grp * 512:(grp + 1) * 512], bsel[:],
                            X1L[:, grp * 512:(grp + 1) * 512], ALU.add,
                            waits=[tms, xm] + (st_marks[-1:] if st_marks else []),
                            inc=1))
        prevv = evs[-1]
        st_marks.append(p.dma(out[tc * 128:(tc + 1) * 128, :], OT[:],
                              waits=[prevv]))
    p.op("sync", lambda h: h.nop(), waits=[st_marks[-1]])
    es.close()


def kernel(**inputs):
    x = np.ascontiguousarray(inputs["L0_attn_x"])
    consts = host_constants()
    base = {
        "log_dt": np.ascontiguousarray(inputs["L0_attn_log_dt"]),
        "log_A_real": np.ascontiguousarray(inputs["L0_attn_log_A_real"]),
        "A_imag": np.ascontiguousarray(inputs["L0_attn_A_imag"]),
        "C_real": np.ascontiguousarray(inputs["L0_attn_C_real"]),
        "C_imag": np.ascontiguousarray(inputs["L0_attn_C_imag"]),
        "Wssm": np.ascontiguousarray(inputs["L0_attn_W"]),
        "bssm": np.ascontiguousarray(inputs["L0_attn_b"]),
        "Wqkv": np.ascontiguousarray(inputs["L0_ffn_Wqkv"]),
        "q_scale": np.ascontiguousarray(inputs["L0_ffn_q_scale"]),
        "k_scale": np.ascontiguousarray(inputs["L0_ffn_k_scale"]),
        "Wo": np.ascontiguousarray(inputs["L0_ffn_Wo"]),
        "ln_g": np.ascontiguousarray(inputs["L0_ffn_ln_g"]),
        "ln_b": np.ascontiguousarray(inputs["L0_ffn_ln_b"]),
        "W1": np.ascontiguousarray(inputs["L0_ffn_W1"]),
        "b1": np.ascontiguousarray(inputs["L0_ffn_b1"]),
        "W2": np.ascontiguousarray(inputs["L0_ffn_W2"]),
        "b2": np.ascontiguousarray(inputs["L0_ffn_b2"]),
    }
    base.update(consts)
    nc = build_program()
    in_maps = []
    for b in range(NCORES):
        m = dict(base)
        m["x"] = np.ascontiguousarray(x[b])
        in_maps.append(m)
    res = run_bass_kernel_spmd(nc, in_maps, core_ids=list(range(NCORES)))
    return np.stack([np.asarray(r["out"]) for r in res.results], axis=0)
